# version 16
# baseline (speedup 1.0000x reference)
import math
from contextlib import ExitStack

import numpy as np

import concourse.bass as bass
import concourse.mybir as mybir
from concourse.bass_utils import run_bass_kernel_spmd

F32 = mybir.dt.float32
BF16 = mybir.dt.bfloat16
FP8 = mybir.dt.float8e4
ALU = mybir.AluOpType
ACT = mybir.ActivationFunctionType
AXX = mybir.AxisListType.X

NCORES = 8
B = 2
SEQ = 4096
D = 4096
NH = 32
DH = 128
TOK = 1024
NTT = TOK // 128
KTOP = 819
ISC = 1.0 / math.sqrt(DH)
NPROBE = 5
G = 4
WSL = D // NCORES
DBG = False


class Emit:
    ENGINES = ("sync", "gpsimd", "vector", "scalar", "tensor")

    def __init__(self, nc):
        self.nc = nc
        self.ops = {e: [] for e in self.ENGINES}
        self.ctr = {}
        self.sems = {}

    def on(self, eng, fn):
        self.ops[eng].append(fn)

    def cur(self, sem):
        return self.ctr.get(sem, 0)

    def dma(self, eng, out, in_, sem):
        s = self.sems[sem]
        self.ctr[sem] = self.cur(sem) + 16
        tgt = self.ctr[sem]
        self.on(eng, lambda e, out=out, in_=in_, s=s: e.dma_start(out=out, in_=in_).then_inc(s, 16))
        return tgt

    def wait(self, eng, sem, tgt):
        if not tgt:
            return
        s = self.sems[sem]
        self.on(eng, lambda e, s=s, tgt=tgt: e.wait_ge(s, tgt))

    def vd(self):
        self.on("vector", lambda e: e.drain())

    def trun_drain(self, fn, sem="tE"):
        s = self.sems[sem]
        self.ctr[sem] = self.cur(sem) + 1
        tgt = self.ctr[sem]
        def wrapped(e, fn=fn, s=s):
            fn(e)
            e.drain().then_inc(s, 1)
        self.on("tensor", wrapped)
        return tgt

    def barrier(self, eng, sems=("tE", "vE", "sE", "gS")):
        for s in sems:
            self.wait(eng, s, self.cur(s))

    def run(self, eng, fn, sem=None):
        if sem is None:
            self.on(eng, fn)
            return None
        s = self.sems[sem]
        self.ctr[sem] = self.cur(sem) + 1
        tgt = self.ctr[sem]
        if eng in ("vector", "scalar"):
            def wrapped(e, fn=fn, s=s):
                fn(e)
                e.drain().then_inc(s, 1)
            self.on(eng, wrapped)
        else:
            self.on(eng, lambda e, fn=fn, s=s: fn(e).then_inc(s, 1))
        return tgt


def build_nc():
    nd = D // 128
    ngrp = nd // 8
    nkc = SEQ // 512
    nkv = SEQ // 128
    ndo = D // 512
    nc = bass.Bass("TRN2", target_bir_lowering=False, detect_race_conditions=False)

    x_hbm = nc.dram_tensor("x_slab", [TOK, D], F32, kind="ExternalInput")
    w_sl = {m: nc.dram_tensor(f"w{m}_sl", [WSL, D], F32, kind="ExternalInput")
            for m in ("q", "k", "v", "o", "f")}
    b_col = {m: nc.dram_tensor(f"b{m}_col", [D, 1], F32, kind="ExternalInput")
             for m in ("q", "k", "v")}
    bo_row = nc.dram_tensor("bo_row", [1, D], F32, kind="ExternalInput")
    bf_row = nc.dram_tensor("bf_row", [1, D], F32, kind="ExternalInput")
    out_hbm = nc.dram_tensor("out_slab", [TOK, D], F32, kind="ExternalOutput")

    wT_in = {m: nc.dram_tensor(f"w{m}T_in", [D, WSL], BF16) for m in ("q", "k", "v", "o", "f")}
    wT_out = {m: nc.dram_tensor(f"w{m}T_out", [NCORES * D, WSL], BF16) for m in ("q", "k", "v", "o", "f")}
    kT_in = nc.dram_tensor("kT_in", [D, TOK], BF16)
    kT_out = nc.dram_tensor("kT_out", [4 * D, TOK], BF16)
    v_in = nc.dram_tensor("v_in", [D, TOK], BF16)
    v_out = nc.dram_tensor("v_out", [4 * D, TOK], BF16)
    ctxT_hbm = nc.dram_tensor("ctxT_hbm", [D, TOK], BF16)
    qxT_hbm = nc.dram_tensor("qxT_hbm", [D, TOK], BF16)
    red_hbm = nc.dram_tensor("red_hbm", [1, 128], F32)
    am_in = nc.dram_tensor("am_in", [1, 8], F32)
    am_out = nc.dram_tensor("am_out", [1, 8], F32)
    ax_in = nc.dram_tensor("ax_in", [1, 8], F32)
    ax_out = nc.dram_tensor("ax_out", [1, 8], F32)

    em = Emit(nc)

    dbg = {}

    def tap(name, ap, shape, dtype, eng="sync", waits=()):
        if not DBG:
            return
        t = nc.dram_tensor(f"dbg_{name}", shape, dtype, kind="ExternalOutput")
        dbg[name] = t
        for sem, tgt in waits:
            em.wait(eng, sem, tgt)
        em.dma(eng, t.ap(), ap, "dOut")

    with ExitStack() as st:
        ec = st.enter_context
        for name in ("dW", "dA", "dB", "dK", "dV", "dO", "dOut", "gS", "cc",
                     "tE", "vE", "sE", "gE"):
            em.sems[name] = ec(nc.semaphore(name))

        ident = ec(nc.sbuf_tensor("ident", [128, 128], BF16))
        iota_r = ec(nc.sbuf_tensor("iota_r", [128, 128], F32))
        iota_c = ec(nc.sbuf_tensor("iota_c", [128, 128], F32))
        ones_row = ec(nc.sbuf_tensor("ones_row", [1, 128], BF16))
        bqkv = {m: ec(nc.sbuf_tensor(f"b{m}_sb", [128, NH], F32)) for m in ("q", "k", "v")}
        bo_sb = ec(nc.sbuf_tensor("bo_sb", [1, D], BF16))
        bfr_sb = ec(nc.sbuf_tensor("bfr_sb", [1, D], F32))
        bfs_sb = ec(nc.sbuf_tensor("bfs_sb", [1, D], BF16))
        sw_sc = ec(nc.sbuf_tensor("sw_sc", [128, 1], F32))
        swi2 = ec(nc.sbuf_tensor("swi2", [128, 1], F32))
        sxi2 = ec(nc.sbuf_tensor("sxi2", [128, 1], F32))
        dq_sc = ec(nc.sbuf_tensor("dq_sc", [128, 1], F32))
        red_sb = ec(nc.sbuf_tensor("red_sb", [1, 128], F32))
        one_sc = ec(nc.sbuf_tensor("one_sc", [1, 8], F32))

        em.run("gpsimd", lambda g: g.iota(iota_r[:], pattern=[[0, 128]], base=0,
                                          channel_multiplier=1, allow_small_or_imprecise_dtypes=True))
        t_iota = em.run("gpsimd", lambda g: g.iota(iota_c[:], pattern=[[1, 128]], base=0,
                                                   channel_multiplier=0,
                                                   allow_small_or_imprecise_dtypes=True), sem="gE")
        em.wait("vector", "gE", t_iota)
        em.run("vector", lambda v: v.tensor_tensor(out=ident[:], in0=iota_r[:], in1=iota_c[:],
                                                   op=ALU.is_equal))
        em.run("vector", lambda v: v.memset(ones_row[:], 1.0))

        for m in ("q", "k", "v"):
            for h in range(NH):
                em.dma("sync", bqkv[m][:, h:h + 1], b_col[m][128 * h:128 * (h + 1), 0:1], "dB")
        t_bias = em.dma("sync", bfr_sb[:], bf_row.ap(), "dB")
        em.wait("scalar", "dB", t_bias)
        em.wait("vector", "dB", t_bias)

        cc_tgt = {}
        bo_cast_v = None
        with ExitStack() as stW:
            ecW = stW.enter_context
            wrow = [ecW(nc.sbuf_tensor(f"wrow{i}", [128, D], F32)) for i in range(2)]
            wbf = ecW(nc.sbuf_tensor("wbf", [128, D], BF16))
            wq8 = ecW(nc.sbuf_tensor("wq8", [128, D], FP8))
            wTsb = ecW(nc.sbuf_tensor("wTsb", [128, 32 * WSL], BF16))
            pW = [ecW(nc.psum_tensor("pW%d" % i, [128, 1024], BF16)) for i in range(2)]
            amw = ecW(nc.sbuf_tensor("amw", [128, 1], F32))
            amw2 = ecW(nc.sbuf_tensor("amw2", [128, 1], F32))
            bo_f32 = ecW(nc.sbuf_tensor("bo_f32", [1, D], F32))

            t_bo = em.dma("sync", bo_f32[:], bo_row.ap(), "dB")
            em.wait("vector", "dB", t_bo)
            bo_cast_v = em.run("vector", lambda v: v.tensor_copy(out=bo_sb[:], in_=bo_f32[:]), sem="vE")

            wrow_free = [None, None]

            def load_wrow(m, j):
                buf = wrow[j % 2]
                em.wait("sync", "vE", wrow_free[j % 2])
                t_l = em.dma("sync", buf[:], w_sl[m][128 * j:128 * (j + 1), :], "dW")
                em.wait("vector", "dW", t_l)
                return buf

            def preprocess_weight(m, quant):
                if quant:
                    t_c = None
                    for j in range(WSL // 128):
                        buf = load_wrow(m, j)
                        if j == 0:
                            t_c = em.run("vector", lambda v, b=buf: v.tensor_reduce(
                                out=amw[:], in_=b[:], axis=AXX, op=ALU.max,
                                apply_absolute_value=True), sem="vE")
                        else:
                            em.run("vector", lambda v, b=buf: v.tensor_reduce(
                                out=amw2[:], in_=b[:], axis=AXX, op=ALU.max,
                                apply_absolute_value=True))
                            em.vd()
                            t_c = em.run("vector", lambda v: v.tensor_tensor(
                                out=amw[:], in0=amw[:], in1=amw2[:], op=ALU.max), sem="vE")
                            em.vd()
                        wrow_free[j % 2] = t_c
                    em.wait("gpsimd", "vE", t_c)
                    t_g = em.dma("gpsimd", red_hbm.ap(), amw[:], "gS")
                    em.wait("gpsimd", "gS", t_g)
                    t_g = em.dma("gpsimd", red_sb[:], red_hbm.ap(), "gS")
                    em.wait("vector", "gS", t_g)
                    em.run("vector", lambda v: v.tensor_reduce(
                        out=one_sc[0:1, 0:1], in_=red_sb[0:1, :], axis=AXX, op=ALU.max))
                    em.vd()
                    em.run("vector", lambda v: v.tensor_copy(
                        out=one_sc[0:1, 0:8], in_=one_sc[0:1, 0:1].to_broadcast([1, 8])))
                    t_v = em.run("vector", lambda v: v.engine_nop(), sem="vE")
                    em.wait("gpsimd", "vE", t_v)
                    t_g = em.dma("gpsimd", am_in[0:1, 0:8], one_sc[0:1, 0:8], "gS")
                    em.wait("gpsimd", "gS", t_g)
                    t_cc = em.run("gpsimd", lambda g: g.collective_compute(
                        "AllReduce", ALU.max, replica_groups=[list(range(8))],
                        ins=[am_in.ap().opt()], outs=[am_out.ap().opt()]), sem="cc")
                    em.wait("gpsimd", "cc", t_cc)
                    t_g = em.dma("gpsimd", sw_sc[:], am_out[0:1, 0:1].to_broadcast([128, 1]), "gS")
                    em.wait("vector", "gS", t_g)
                    em.run("vector", lambda v: v.tensor_scalar(
                        out=sw_sc[:], in0=sw_sc[:], scalar1=1e-12, scalar2=1.0 / 448.0,
                        op0=ALU.max, op1=ALU.mult))
                    em.vd()
                    em.run("vector", lambda v: v.tensor_scalar(
                        out=swi2[:], in0=sw_sc[:], scalar1=2.0, scalar2=None, op0=ALU.mult))
                    em.vd()
                    em.run("vector", lambda v: v.reciprocal(out=swi2[:], in_=swi2[:]))
                    em.vd()

                ag_guard = em.cur("gS")
                for j in range(WSL // 128):
                    buf = load_wrow(m, j)
                    if quant:
                        em.run("vector", lambda v, b=buf: v.tensor_scalar(
                            out=wq8[:], in0=b[:], scalar1=swi2[:], scalar2=None, op0=ALU.mult))
                        em.vd()
                        t_c = em.run("vector", lambda v: v.tensor_copy(out=wbf[:], in_=wq8[:]), sem="vE")
                    else:
                        t_c = em.run("vector", lambda v, b=buf: v.tensor_copy(out=wbf[:], in_=b[:]), sem="vE")
                    wrow_free[j % 2] = t_c
                    em.wait("tensor", "vE", t_c)
                    if j == 0:
                        em.wait("vector", "gS", ag_guard)
                    for grp in range(ngrp):
                        pt = pW[grp % 2]
                        for k in range(8):
                            i = grp * 8 + k
                            em.run("tensor", lambda t, i=i, k=k, pt=pt: t.transpose(
                                out=pt[:, 128 * k:128 * (k + 1)], in_=wbf[:, 128 * i:128 * (i + 1)],
                                identity=ident[:]))
                        t_t = em.trun_drain(lambda t: t.nop() if hasattr(t, "nop") else t.engine_nop())
                        em.wait("vector", "tE", t_t)
                        t_e = em.run("vector", lambda v, pt=pt, grp=grp, j=j: v.tensor_copy(
                            out=wTsb[:, :].rearrange("p (c w) -> p c w", w=WSL)
                            [:, grp * 8:grp * 8 + 8, 128 * j:128 * (j + 1)],
                            in_=pt[:].rearrange("p (c w) -> p c w", w=128)), sem="vE")
                        em.wait("tensor", "vE", t_e)
                t_v = em.run("vector", lambda v: v.engine_nop(), sem="vE")
                em.wait("gpsimd", "vE", t_v)
                t_g = None
                for half in range(2):
                    hh = nd // 2
                    t_g = em.dma("gpsimd",
                                 wT_in[m].ap().rearrange("(c p) w -> p c w", p=128)
                                 [:, hh * half:hh * (half + 1)],
                                 wTsb[:, :].rearrange("p (c w) -> p c w", w=WSL)
                                 [:, hh * half:hh * (half + 1)], "gS")
                em.wait("gpsimd", "gS", t_g)
                t_ag = em.run("gpsimd", lambda g, m=m: g.collective_compute(
                    "AllGather", ALU.bypass, replica_groups=[list(range(8))],
                    ins=[wT_in[m].ap().opt()], outs=[wT_out[m].ap().opt()]), sem="cc")
                em.wait("gpsimd", "cc", t_ag)
                return t_ag

            for m in ("q", "k", "v", "o"):
                cc_tgt[m] = preprocess_weight(m, False)
            cc_tgt["f"] = preprocess_weight("f", True)

        st01 = ExitStack()
        qT_sb = st01.enter_context(nc.sbuf_tensor("qT_sb", [128, NH * TOK], BF16))
        with ExitStack() as st0:
            ec0 = st0.enter_context
            xrow = [ec0(nc.sbuf_tensor("xrow0", [128, D], F32))]
            xbf = ec0(nc.sbuf_tensor("xbf", [128, D], BF16))
            xT_sb = ec0(nc.sbuf_tensor("xT_sb", [128, 32 * TOK], BF16))
            p0 = [ec0(nc.psum_tensor("p0_%d" % i, [128, 1024], BF16)) for i in range(2)]
            wstrip = [ec0(nc.sbuf_tensor(f"wstrip{i}", [128, D], BF16)) for i in range(2)]
            pp = [ec0(nc.psum_tensor("pp%d" % i, [128, 512], F32)) for i in range(2)]
            kvrow = ec0(nc.sbuf_tensor("kvrow", [128, 1024], BF16))

            em.barrier("sync")
            em.barrier("gpsimd")
            xrow_free = [None]
            for j in range(NTT):
                buf = xrow[0]
                em.wait("sync", "vE", xrow_free[0])
                t_l = em.dma("sync", buf[:], x_hbm[128 * j:128 * (j + 1), :], "dA")
                em.wait("vector", "dA", t_l)
                t_c = em.run("vector", lambda v, b=buf: v.tensor_copy(out=xbf[:], in_=b[:]), sem="vE")
                xrow_free[0] = t_c
                em.wait("tensor", "vE", t_c)
                for grp in range(ngrp):
                    pt = p0[grp % 2]
                    for k in range(8):
                        i = grp * 8 + k
                        em.run("tensor", lambda t, i=i, k=k, pt=pt: t.transpose(
                            out=pt[:, 128 * k:128 * (k + 1)], in_=xbf[:, 128 * i:128 * (i + 1)],
                            identity=ident[:]))
                    t_t = em.trun_drain(lambda t: t.nop() if hasattr(t, "nop") else t.engine_nop())
                    em.wait("vector", "tE", t_t)
                    t_e = em.run("vector", lambda v, pt=pt, grp=grp, j=j: v.tensor_copy(
                        out=xT_sb[:, :].rearrange("p (c w) -> p c w", w=TOK)
                        [:, grp * 8:grp * 8 + 8, 128 * j:128 * (j + 1)],
                        in_=pt[:].rearrange("p (c w) -> p c w", w=128)), sem="vE")
                    em.wait("tensor", "vE", t_e)

            tap("xT", xT_sb[:, 0:512], [128, 512], BF16, waits=[("vE", em.cur("vE"))])
            em.wait("sync", "cc", cc_tgt["v"])
            wstrip_free = [None, None]

            def proj(m, sink):
                for c in range(nd):
                    blk, col = (c * 128) // WSL, (c * 128) % WSL
                    sb = wstrip[c % 2]
                    em.wait("sync", "tE", wstrip_free[c % 2])
                    t_l = None
                    for qtr in range(2):
                        hh = nd // 2
                        t_l = em.dma("sync",
                                     sb[:, :].rearrange("p (ci w) -> p ci w", w=128)
                                     [:, hh * qtr:hh * (qtr + 1)],
                                     wT_out[m].ap()[blk * D:(blk + 1) * D, col:col + 128]
                                     .rearrange("(ci p) w -> p ci w", p=128)
                                     [:, hh * qtr:hh * (qtr + 1)], "dA")
                    em.wait("tensor", "dA", t_l)
                    for half in range(TOK // 512 if TOK >= 512 else 1):
                        psum = pp[half]
                        hw = min(512, TOK)
                        t_mm = None
                        for di in range(nd):
                            t_mm = em.run("tensor", lambda t, sb=sb, di=di, half=half, psum=psum, hw=hw: t.matmul(
                                out=psum[:, 0:hw], lhsT=sb[:, 128 * di:128 * (di + 1)],
                                rhs=xT_sb[:, di * TOK + hw * half: di * TOK + hw * half + hw],
                                start=(di == 0), stop=(di == nd - 1)), sem="tE")
                        wstrip_free[c % 2] = t_mm
                        sink(c, half, psum, t_mm)

            def q_sink(c, half, psum, t_mm):
                em.wait("vector", "tE", t_mm)
                hw = min(512, TOK)
                t_s = em.run("vector", lambda v, psum=psum, c=c, half=half, hw=hw: v.tensor_scalar(
                    out=qT_sb[:, c * TOK + hw * half: c * TOK + hw * (half + 1)],
                    in0=psum[:, 0:hw], scalar1=bqkv["q"][:, c:c + 1], scalar2=None, op0=ALU.add), sem="vE")
                em.wait("tensor", "vE", t_s)
            proj("q", q_sink)
            tap("wstrip", wstrip[1][:, 0:512], [128, 512], BF16, waits=[("tE", em.cur("tE"))])
            tap("qT", qT_sb[:, 0:512], [128, 512], BF16, waits=[("vE", em.cur("vE"))])

            def k_sink(c, half, psum, t_mm):
                em.wait("vector", "tE", t_mm)
                em.wait("vector", "gS", em.cur("gS"))
                hw = min(512, TOK)
                t_s = em.run("vector", lambda v, psum=psum, c=c, hw=hw: v.tensor_scalar(
                    out=kvrow[:, 0:hw], in0=psum[:, 0:hw], scalar1=bqkv["k"][:, c:c + 1],
                    scalar2=None, op0=ALU.add), sem="vE")
                em.wait("tensor", "vE", t_s)
                em.wait("gpsimd", "vE", t_s)
                em.dma("gpsimd", kT_in.ap()[c * 128:(c + 1) * 128, hw * half:hw * (half + 1)],
                       kvrow[:, 0:hw], "gS")
            proj("k", k_sink)

            def v_sink(c, half, psum, t_mm):
                em.wait("vector", "tE", t_mm)
                em.wait("vector", "gS", em.cur("gS"))
                hw = min(512, TOK)
                t_s = em.run("vector", lambda v, psum=psum, hw=hw: v.tensor_scalar(
                    out=kvrow[:, 0:hw], in0=psum[:, 0:hw], scalar1=0.0, scalar2=None,
                    op0=ALU.add), sem="vE")
                em.wait("gpsimd", "vE", t_s)
                em.dma("gpsimd", v_in.ap()[c * 128:(c + 1) * 128, hw * half:hw * (half + 1)],
                       kvrow[:, 0:hw], "gS")
            proj("v", v_sink)
            em.wait("gpsimd", "gS", em.cur("gS"))
            cc_k = em.run("gpsimd", lambda g: g.collective_compute(
                "AllGather", ALU.bypass, replica_groups=[[0, 1, 2, 3], [4, 5, 6, 7]],
                ins=[kT_in.ap().opt()], outs=[kT_out.ap().opt()]), sem="cc")
            tap("vin", v_in.ap(), [D, TOK], BF16, eng="gpsimd", waits=[("gS", em.cur("gS"))])
            cc_v = em.run("gpsimd", lambda g: g.collective_compute(
                "AllGather", ALU.bypass, replica_groups=[[0, 1, 2, 3], [4, 5, 6, 7]],
                ins=[v_in.ap().opt()], outs=[v_out.ap().opt()]), sem="cc")
            em.wait("gpsimd", "cc", cc_v)

        gG = min(G, NTT)
        with ExitStack() as st1:
            ec1 = st1.enter_context
            kT_h = ec1(nc.sbuf_tensor("kT_h", [128, SEQ], BF16))
            v_h = ec1(nc.sbuf_tensor("v_h", [128, SEQ], BF16))
            s_g = [ec1(nc.sbuf_tensor(f"s_g{i}", [128, SEQ], F32)) for i in range(min(G, NTT))]
            es2 = [ec1(nc.sbuf_tensor(f"es2_{i}", [128, SEQ], BF16)) for i in range(2)]
            zpT = ec1(nc.sbuf_tensor("zpT", [128, SEQ], BF16))
            ctx_t = [ec1(nc.sbuf_tensor(f"ctx_t{i}", [128, 128], BF16)) for i in range(2)]
            qcols = 512 * max(1, (SEQ // 8) // 512)
            sqd = ec1(nc.sbuf_tensor("sqd", [128, qcols], BF16))
            ksum = ec1(nc.sbuf_tensor("ksum", [128, 2], F32))
            musq = ec1(nc.sbuf_tensor("musq", [128, 2 * gG], F32))
            ksumb = ec1(nc.sbuf_tensor("ksumb", [128, 512], BF16))
            stt = {n: ec1(nc.sbuf_tensor(f"st_{n}", [128, gG],
                                         mybir.dt.int32 if n in ("ge", "lt") else F32))
                   for n in ("lo", "hi", "mid", "cnt", "ge", "lt", "mu", "sd", "sum", "s2", "mq")}
            ntg = ec1(nc.sbuf_tensor("ntg", [128, gG], F32))
            zrow = ec1(nc.sbuf_tensor("zrow", [128, gG], F32))
            rz = ec1(nc.sbuf_tensor("rz", [128, 1], F32))
            lo_all = ec1(nc.sbuf_tensor("lo_all", [128, NH * NTT], F32)) if DBG else None
            z_all = ec1(nc.sbuf_tensor("z_all", [128, NH * NTT], F32)) if DBG else None
            p_s = [ec1(nc.psum_tensor("p1s%d" % i, [128, 512], F32)) for i in range(4)]
            p_t2 = [ec1(nc.psum_tensor("p1t%d" % i, [128, 1024], BF16)) for i in range(2)]
            p_o = [ec1(nc.psum_tensor("p1o%d" % i, [128, 128], F32)) for i in range(2)]

            em.wait("sync", "cc", cc_v)
            em.barrier("sync")
            em.barrier("gpsimd")
            em.run("vector", lambda v: v.memset(ksumb[:], 0.0))
            em.vd()
            tap("vout", v_out.ap(), [4 * D, TOK], BF16, eng="gpsimd", waits=[("cc", cc_v)])

            npk = min(8, nkv)
            ngv = max(1, nkv // 8)
            ngq = NTT // gG
            st_p1 = {"kq_done": None, "t_k": None, "t_v": None}
            tr_prev = [None, None]
            ctx_dma = [None, None]

            def load_kv(h):
                em.wait("sync", "tE", st_p1["kq_done"])
                t_k = None
                for r in range(4):
                    t_k = em.dma("sync", kT_h[:, TOK * r:TOK * (r + 1)],
                                 kT_out.ap()[r * D + h * 128: r * D + (h + 1) * 128, :], "dK")
                st_p1["t_k"] = t_k
                em.wait("sync", "tE", tr_prev[0])
                t_v = None
                for r in range(4):
                    t_v = em.dma("sync", es2[0][:, TOK * r:TOK * (r + 1)],
                                 v_out.ap()[r * D + h * 128: r * D + (h + 1) * 128, :], "dV")
                st_p1["t_v"] = t_v

            def stage_v(h):
                em.wait("tensor", "dV", st_p1["t_v"])
                ev = [None, None]
                for grp in range(ngv):
                    if ev[grp % 2] is not None:
                        em.wait("tensor", "vE", ev[grp % 2])
                    for k in range(npk):
                        kc = grp * npk + k
                        em.run("tensor", lambda t, kc=kc, k=k, grp=grp: t.transpose(
                            out=p_t2[grp % 2][:, 128 * k:128 * (k + 1)],
                            in_=es2[0][:, 128 * kc:128 * (kc + 1)], identity=ident[:]))
                    t_t2 = em.trun_drain(lambda t: t.nop() if hasattr(t, "nop") else t.engine_nop())
                    em.wait("vector", "tE", t_t2)
                    ev[grp % 2] = em.run("vector", lambda v, grp=grp: v.tensor_copy(
                        out=v_h[:, 128 * npk * grp:128 * npk * (grp + 1)],
                        in_=p_t2[grp % 2][:, 0:128 * npk]), sem="vE")
                em.wait("tensor", "vE", ev[(ngv - 1) % 2])
                if ngv > 1:
                    em.wait("tensor", "vE", ev[ngv % 2])
                tr_prev[0] = em.cur("tE")
                if h == 0:
                    tap("kTh", kT_h[:, 0:512], [128, 512], BF16, waits=[("dK", st_p1["t_k"])])
                    tap("vh", v_h[:, 0:512], [128, 512], BF16, waits=[("vE", em.cur("vE"))])

            def emit_scores(h, gq, guards):
                em.wait("tensor", "dK", st_p1["t_k"])
                if gq == 0:
                    em.wait("vector", "dK", st_p1["t_k"])
                    em.run("vector", lambda v: v.tensor_reduce(
                        out=ksum[:, 0:1], in_=kT_h[:, 0:SEQ], axis=AXX, op=ALU.add))
                    em.run("vector", lambda v: v.tensor_reduce(
                        out=ksum[:, 1:2], in_=kT_h[:, 0:qcols], axis=AXX, op=ALU.add))
                    em.vd()
                    t_ks = em.run("vector", lambda v: v.tensor_copy(
                        out=ksumb[:, 0:2], in_=ksum[:]), sem="vE")
                    em.wait("tensor", "vE", t_ks)
                nc4 = min(4, nkc)
                spl = max(1, nc4 // 2)
                for g in range(gG):
                    qt = gq * gG + g
                    kc0_evac = None
                    for rnd in range(max(1, nkc // 4)):
                        if rnd == 0:
                            em.wait("tensor", "vE", st_p1.get("musq_cp"))
                        t_mm = None
                        for kc4 in range(nc4):
                            kc = rnd * nc4 + kc4
                            t_mm = em.run("tensor", lambda t, h=h, qt=qt, kc=kc, kc4=kc4: t.matmul(
                                out=p_s[kc4][:],
                                lhsT=qT_sb[:, h * TOK + 128 * qt: h * TOK + 128 * (qt + 1)],
                                rhs=kT_h[:, 512 * kc:512 * (kc + 1)],
                                start=True, stop=True), sem="tE")
                        em.wait("scalar", "tE", t_mm)
                        em.wait("vector", "tE", t_mm)
                        if rnd == 0 and guards is not None:
                            em.wait("scalar", "vE", guards[0][g])
                            em.wait("vector", "sE", guards[1][g])
                        t_s = None
                        t_v = None
                        for kc4 in range(nc4):
                            kc = rnd * nc4 + kc4
                            if kc4 < spl:
                                t_v = em.run("vector", lambda v, g=g, kc=kc, kc4=kc4: v.tensor_copy(
                                    out=s_g[g][:, 512 * kc:512 * (kc + 1)], in_=p_s[kc4][:]),
                                    sem="vE")
                                if kc == 0:
                                    kc0_evac = t_v
                            else:
                                t_s = em.run("scalar", lambda s, g=g, kc=kc, kc4=kc4: s.activation(
                                    out=s_g[g][:, 512 * kc:512 * (kc + 1)], in_=p_s[kc4][:],
                                    func=ACT.Copy), sem="sE")
                        if t_s is not None:
                            em.wait("tensor", "sE", t_s)
                        em.wait("tensor", "vE", t_v)
                    t_mu = em.run("tensor", lambda t, h=h, qt=qt: t.matmul(
                        out=p_s[0][:],
                        lhsT=qT_sb[:, h * TOK + 128 * qt: h * TOK + 128 * (qt + 1)],
                        rhs=ksumb[:], start=True, stop=True), sem="tE")
                    em.wait("vector", "tE", t_mu)
                    st_p1["musq_cp"] = em.run("vector", lambda v, g=g: v.tensor_copy(
                        out=musq[:, 2 * g:2 * g + 2], in_=p_s[0][:, 0:2]), sem="vE")
                    em.wait("scalar", "vE", kc0_evac)
                    sm = "sE" if g == gG - 1 else None
                    t_sq = em.run("scalar", lambda s, g=g: s.activation(
                        out=sqd[:], in_=s_g[g][:, 0:qcols], func=ACT.Square,
                        accum_out=stt["s2"][:, g:g + 1]), sem=sm)
                st_p1["kq_done"] = em.cur("tE")
                return t_sq

            def emit_stats(t_sq, first):
                em.wait("vector", "sE", t_sq)
                for g in range(gG):
                    em.run("vector", lambda v, g=g: v.tensor_copy(
                        out=stt["sum"][:, g:g + 1], in_=musq[:, 2 * g:2 * g + 1]))
                    em.run("vector", lambda v, g=g: v.tensor_copy(
                        out=stt["mq"][:, g:g + 1], in_=musq[:, 2 * g + 1:2 * g + 2]))
                em.vd()
                if first:
                    tap("s0", s_g[0][:, 0:SEQ], [128, SEQ], F32, waits=[("sE", t_sq)])
                em.run("vector", lambda v: v.tensor_scalar(
                    out=stt["mu"][:], in0=stt["sum"][:], scalar1=1.0 / SEQ, scalar2=None,
                    op0=ALU.mult))
                em.run("vector", lambda v: v.tensor_scalar(
                    out=stt["mq"][:], in0=stt["mq"][:], scalar1=1.0 / qcols, scalar2=None,
                    op0=ALU.mult))
                em.vd()
                em.run("vector", lambda v: v.tensor_tensor(
                    out=stt["mid"][:], in0=stt["mq"][:], in1=stt["mq"][:], op=ALU.mult))
                em.run("vector", lambda v: v.tensor_scalar(
                    out=stt["sd"][:], in0=stt["s2"][:], scalar1=1.0 / qcols, scalar2=None,
                    op0=ALU.mult))
                em.vd()
                em.run("vector", lambda v: v.tensor_tensor(
                    out=stt["sd"][:], in0=stt["sd"][:], in1=stt["mid"][:], op=ALU.subtract))
                em.vd()
                t_v = em.run("vector", lambda v: v.tensor_scalar(
                    out=stt["sd"][:], in0=stt["sd"][:], scalar1=1e-12, scalar2=None,
                    op0=ALU.max), sem="vE")
                em.wait("scalar", "vE", t_v)
                t_s = em.run("scalar", lambda s: s.activation(
                    out=stt["sd"][:], in_=stt["sd"][:], func=ACT.Sqrt), sem="sE")
                em.wait("vector", "sE", t_s)
                em.run("vector", lambda v: v.tensor_scalar(
                    out=stt["lo"][:], in0=stt["sd"][:], scalar1=0.65, scalar2=None, op0=ALU.mult))
                em.run("vector", lambda v: v.tensor_scalar(
                    out=stt["hi"][:], in0=stt["sd"][:], scalar1=1.05, scalar2=None, op0=ALU.mult))
                em.vd()
                em.run("vector", lambda v: v.tensor_tensor(
                    out=stt["lo"][:], in0=stt["lo"][:], in1=stt["mu"][:], op=ALU.add))
                em.run("vector", lambda v: v.tensor_tensor(
                    out=stt["hi"][:], in0=stt["hi"][:], in1=stt["mu"][:], op=ALU.add))
                em.vd()

            def emit_probes():
                gsp = gG // 2
                for it in range(NPROBE):
                    em.run("vector", lambda v: v.tensor_tensor(
                        out=stt["mid"][:], in0=stt["lo"][:], in1=stt["hi"][:], op=ALU.add))
                    em.vd()
                    t_mid = em.run("vector", lambda v: v.tensor_scalar(
                        out=stt["mid"][:], in0=stt["mid"][:], scalar1=0.5, scalar2=None,
                        op0=ALU.mult), sem="vE")
                    t_scnt = None
                    if gsp > 0:
                        em.wait("scalar", "vE", t_mid)
                        for g in range(gG - gsp, gG):
                            em.run("scalar", lambda s, g=g: s.activation(
                                out=es2[1][:], in_=s_g[g][:], func=ACT.Sign,
                                bias=stt["mid"][:, g:g + 1], scale=-1.0,
                                accum_out=stt["sum"][:, g:g + 1]))
                        t_scnt = em.run("scalar", lambda s, gsp=gsp: s.activation(
                            out=stt["cnt"][:, gG - gsp:gG], in_=stt["sum"][:, gG - gsp:gG],
                            func=ACT.Copy, bias=float(SEQ) / 2.0, scale=-0.5), sem="sE")
                    for g in range(gG - gsp):
                        em.run("vector", lambda v, g=g: v.tensor_scalar(
                            out=es2[0][:], in0=s_g[g][:], scalar1=stt["mid"][:, g:g + 1],
                            scalar2=None, op0=ALU.is_ge, op1=ALU.add,
                            accum_out=stt["cnt"][:, g:g + 1]))
                    em.vd()
                    if t_scnt:
                        em.wait("vector", "sE", t_scnt)
                    em.run("vector", lambda v: v.tensor_scalar(
                        out=stt["ge"][:], in0=stt["cnt"][:], scalar1=float(KTOP),
                        scalar2=None, op0=ALU.is_ge))
                    em.run("vector", lambda v: v.tensor_scalar(
                        out=stt["lt"][:], in0=stt["cnt"][:], scalar1=float(KTOP),
                        scalar2=None, op0=ALU.is_lt))
                    em.vd()
                    em.run("vector", lambda v: v.copy_predicated(
                        out=stt["lo"][:], mask=stt["ge"][:], data=stt["mid"][:]))
                    em.run("vector", lambda v: v.copy_predicated(
                        out=stt["hi"][:], mask=stt["lt"][:], data=stt["mid"][:]))
                    em.vd()

            def emit_epilogue(h, gq, boundary, first):
                t_nt = em.run("vector", lambda v: v.tensor_scalar(
                    out=ntg[:], in0=stt["lo"][:], scalar1=-ISC, scalar2=None,
                    op0=ALU.mult), sem="vE")
                em.wait("scalar", "vE", t_nt)
                if first:
                    tap("lo", stt["lo"][:], [128, gG], F32, waits=[("vE", t_nt)])
                    tap("mu", stt["mu"][:], [128, gG], F32, waits=[("vE", t_nt)])
                    tap("sd", stt["sd"][:], [128, gG], F32, waits=[("vE", t_nt)])
                    tap("cnt", stt["cnt"][:], [128, gG], F32, waits=[("vE", t_nt)])
                exp_done = [None] * gG
                pop_done = [None] * gG
                for g in range(min(2, gG)):
                    if tr_prev[g % 2] is not None:
                        em.wait("scalar", "tE", tr_prev[g % 2])
                    exp_done[g] = em.run("scalar", lambda s, g=g: s.activation(
                        out=es2[g % 2][:], in_=s_g[g][:], func=ACT.Exp,
                        bias=ntg[:, g:g + 1], scale=ISC), sem="sE")
                zpc = [None, None]
                for g in range(gG):
                    qt = gq * gG + g
                    em.wait("vector", "sE", exp_done[g])
                    pop_done[g] = em.run("vector", lambda v, g=g: v.scalar_tensor_tensor(
                        out=es2[g % 2][:], in0=s_g[g][:], scalar=stt["lo"][:, g:g + 1],
                        in1=es2[g % 2][:], op0=ALU.is_ge, op1=ALU.mult,
                        accum_out=zrow[:, g:g + 1]), sem="vE")
                    em.run("vector", lambda v, g=g: v.reciprocal(out=rz[:], in_=zrow[:, g:g + 1]))
                    em.vd()
                    t_norm = em.run("vector", lambda v, g=g: v.tensor_scalar(
                        out=es2[g % 2][:], in0=es2[g % 2][:], scalar1=rz[:], scalar2=None,
                        op0=ALU.mult), sem="vE")
                    pc = st_p1.get("pend_ctx")
                    if pc is not None:
                        ph, pqt, ppar, ppv = pc
                        em.wait("vector", "tE", ppv)
                        if ctx_dma[ppar] is not None:
                            em.wait("vector", "gS", ctx_dma[ppar])
                        t_oe = em.run("vector", lambda v, ph=ph, ppar=ppar: v.tensor_scalar(
                            out=ctx_t[ppar][:], in0=p_o[ppar][:], scalar1=bqkv["v"][:, ph:ph + 1],
                            scalar2=None, op0=ALU.add), sem="vE")
                        st_p1["ctx_vE%d" % ppar] = t_oe
                        em.wait("gpsimd", "vE", t_oe)
                        ctx_dma[ppar] = em.dma(
                            "gpsimd",
                            ctxT_hbm.ap()[ph * 128:(ph + 1) * 128, pqt * 128:(pqt + 1) * 128],
                            ctx_t[ppar][:], "gS")
                    em.wait("tensor", "vE", t_norm)
                    for grp in range(ngv):
                        if zpc[grp % 2] is not None:
                            em.wait("tensor", "sE" if grp % 2 == 0 else "vE", zpc[grp % 2])
                        for k in range(npk):
                            kc = grp * npk + k
                            em.run("tensor", lambda t, kc=kc, k=k, g=g, grp=grp: t.transpose(
                                out=p_t2[grp % 2][:, 128 * k:128 * (k + 1)],
                                in_=es2[g % 2][:, 128 * kc:128 * (kc + 1)], identity=ident[:]))
                        t_t = em.trun_drain(lambda t: t.nop() if hasattr(t, "nop") else t.engine_nop())
                        if grp % 2 == 0:
                            em.wait("scalar", "tE", t_t)
                            zpc[0] = em.run("scalar", lambda s, grp=grp: s.activation(
                                out=zpT[:, 128 * npk * grp:128 * npk * (grp + 1)],
                                in_=p_t2[0][:, 0:128 * npk], func=ACT.Copy), sem="sE")
                        else:
                            em.wait("vector", "tE", t_t)
                            zpc[1] = em.run("vector", lambda v, grp=grp: v.tensor_copy(
                                out=zpT[:, 128 * npk * grp:128 * npk * (grp + 1)],
                                in_=p_t2[1][:, 0:128 * npk]), sem="vE")
                    tr_prev[g % 2] = em.cur("tE")
                    if g + 2 < gG:
                        g2 = g + 2
                        em.wait("scalar", "tE", tr_prev[g % 2])
                        exp_done[g2] = em.run("scalar", lambda s, g2=g2: s.activation(
                            out=es2[g2 % 2][:], in_=s_g[g2][:], func=ACT.Exp,
                            bias=ntg[:, g2:g2 + 1], scale=ISC), sem="sE")
                    if zpc[0] is not None:
                        em.wait("tensor", "sE", zpc[0])
                    if zpc[1] is not None:
                        em.wait("tensor", "vE", zpc[1])
                    cv = st_p1.get("ctx_vE%d" % (g % 2))
                    if cv is not None:
                        em.wait("tensor", "vE", cv)
                    t_mm = None
                    for kc in range(nkv):
                        t_mm = em.run("tensor", lambda t, kc=kc, g=g: t.matmul(
                            out=p_o[g % 2][:], lhsT=v_h[:, 128 * kc:128 * (kc + 1)],
                            rhs=zpT[:, 128 * kc:128 * (kc + 1)],
                            start=(kc == 0), stop=(kc == nkv - 1)), sem="tE")
                    st_p1["pend_ctx"] = (h, qt, g % 2, t_mm)
                    if DBG:
                        em.run("vector", lambda v, h=h, gq=gq, g=g: v.tensor_copy(
                            out=z_all[:, (h * ngq + gq) * gG + g:(h * ngq + gq) * gG + g + 1],
                            in_=zrow[:, g:g + 1]))
                    if first and g == 0:
                        tap("zrow", zrow[:], [128, gG], F32, waits=[("vE", em.cur("vE"))])
                if DBG:
                    em.run("vector", lambda v, h=h, gq=gq: v.tensor_copy(
                        out=lo_all[:, (h * ngq + gq) * gG:(h * ngq + gq + 1) * gG],
                        in_=stt["lo"][:]))
                if boundary and h + 1 < NH:
                    load_kv(h + 1)
                    stage_v(h + 1)
                return pop_done, exp_done

            load_kv(0)
            stage_v(0)
            t_sq = emit_scores(0, 0, None)
            for h in range(NH):
                for gq in range(ngq):
                    first = (h == 0 and gq == 0)
                    emit_stats(t_sq, first)
                    emit_probes()
                    boundary = (gq == ngq - 1)
                    pop_done = emit_epilogue(h, gq, boundary, first)
                    if not (h == NH - 1 and boundary):
                        nh, ngq2 = (h, gq + 1) if not boundary else (h + 1, 0)
                        t_sq = emit_scores(nh, ngq2, pop_done)

            pc = st_p1.get("pend_ctx")
            if pc is not None:
                ph, pqt, ppar, ppv = pc
                em.wait("vector", "tE", ppv)
                if ctx_dma[ppar] is not None:
                    em.wait("vector", "gS", ctx_dma[ppar])
                t_oe = em.run("vector", lambda v, ph=ph, ppar=ppar: v.tensor_scalar(
                    out=ctx_t[ppar][:], in0=p_o[ppar][:], scalar1=bqkv["v"][:, ph:ph + 1],
                    scalar2=None, op0=ALU.add), sem="vE")
                em.wait("gpsimd", "vE", t_oe)
                ctx_dma[ppar] = em.dma(
                    "gpsimd", ctxT_hbm.ap()[ph * 128:(ph + 1) * 128, pqt * 128:(pqt + 1) * 128],
                    ctx_t[ppar][:], "gS")

            if DBG:
                t_fin = em.run("vector", lambda v: v.engine_nop(), sem="vE")
                tap("loall", lo_all[:], [128, NH * NTT], F32, waits=[("vE", t_fin)])
                tap("zall", z_all[:], [128, NH * NTT], F32, waits=[("vE", t_fin)])

        st01.close()
        r_sb = ec(nc.sbuf_tensor("r_sb", [128, NTT * D], F32))
        ctx_done = em.cur("gS")

        with ExitStack() as st2:
            ec2 = st2.enter_context
            lhs_b = [ec2(nc.sbuf_tensor(f"lhs_b{i}", [128, 512], BF16)) for i in range(2)]
            wb1 = [ec2(nc.sbuf_tensor(f"wb1_{i}", [128, 512], BF16)) for i in range(2)]
            wb2 = [ec2(nc.sbuf_tensor(f"wb2_{i}", [128, 512], BF16)) for i in range(2)]
            scr = ec2(nc.sbuf_tensor("scr", [128, D], F32))
            scr8 = ec2(nc.sbuf_tensor("scr8", [128, D], FP8))
            scrb = ec2(nc.sbuf_tensor("scrb", [128, D], BF16))
            mstat = ec2(nc.sbuf_tensor("mstat", [128, NTT], F32))
            rstat = ec2(nc.sbuf_tensor("rstat", [128, NTT], F32))
            amx = ec2(nc.sbuf_tensor("amx", [128, 1], F32))
            amx2 = ec2(nc.sbuf_tensor("amx2", [128, 1], F32))

            def matmul_sweep(lhsT_hbm, bias_sb, wkey, evac, lhs_guard, bias_guard):
                with ExitStack() as stS:
                    pmm = [stS.enter_context(nc.psum_tensor(f"pS_{wkey}{i}", [128, 512], F32))
                           for i in range(8)]
                    em.wait("sync", "gS", lhs_guard)
                    em.barrier("sync", sems=("tE", "vE", "sE"))
                    em.wait("tensor", "vE", bias_guard)
                    lfree = [None, None]
                    wfree = [None, None]
                    nquad = max(1, NTT // 4)
                    ntq = min(4, NTT)
                    npair = NCORES // 2
                    for quad in range(nquad):
                        for pair in range(npair):
                            r1, r2 = 2 * pair, 2 * pair + 1
                            t_mm = None
                            for di in range(nd):
                                par = di % 2
                                em.wait("sync", "tE", lfree[par])
                                em.dma("sync", lhs_b[par][:, 0:128 * ntq],
                                       lhsT_hbm[di * 128:(di + 1) * 128,
                                                quad * 128 * ntq:(quad + 1) * 128 * ntq], "dO")
                                em.wait("sync", "tE", wfree[par])
                                em.dma("sync", wb1[par][:, 0:WSL],
                                       wT_out[wkey].ap()[r1 * D + di * 128: r1 * D + (di + 1) * 128, :],
                                       "dO")
                                t_l = em.dma("sync", wb2[par][:, 0:WSL],
                                             wT_out[wkey].ap()[r2 * D + di * 128: r2 * D + (di + 1) * 128, :],
                                             "dO")
                                em.wait("tensor", "dO", t_l)
                                for tt4 in range(ntq):
                                    em.run("tensor", lambda t, par=par, tt4=tt4, di=di: t.matmul(
                                        out=pmm[2 * tt4][:, 0:WSL],
                                        lhsT=lhs_b[par][:, 128 * tt4:128 * (tt4 + 1)],
                                        rhs=wb1[par][:, 0:WSL], start=(di == 0), stop=False))
                                    t_mm = em.run("tensor", lambda t, par=par, tt4=tt4, di=di: t.matmul(
                                        out=pmm[2 * tt4 + 1][:, 0:WSL],
                                        lhsT=lhs_b[par][:, 128 * tt4:128 * (tt4 + 1)],
                                        rhs=wb2[par][:, 0:WSL], start=(di == 0), stop=False), sem="tE")
                                lfree[par] = t_mm
                                wfree[par] = t_mm
                            for tt4 in range(ntq):
                                em.run("tensor", lambda t, r1=r1, tt4=tt4: t.matmul(
                                    out=pmm[2 * tt4][:, 0:WSL], lhsT=ones_row[:],
                                    rhs=bias_sb[0:1, WSL * r1:WSL * (r1 + 1)],
                                    start=False, stop=True))
                                t_mm = em.run("tensor", lambda t, r2=r2, tt4=tt4: t.matmul(
                                    out=pmm[2 * tt4 + 1][:, 0:WSL], lhsT=ones_row[:],
                                    rhs=bias_sb[0:1, WSL * r2:WSL * (r2 + 1)],
                                    start=False, stop=True), sem="tE")
                            for tt4 in range(ntq):
                                evac(quad * ntq + tt4, pair, pmm[2 * tt4], pmm[2 * tt4 + 1], t_mm)
                    em.wait("tensor", "vE", em.cur("vE"))

            def o_evac(tt, pair, pa, pb, t_mm):
                em.wait("vector", "tE", t_mm)
                t_v = None
                for which, pq in ((0, pa), (1, pb)):
                    dcol = (2 * pair + which) * WSL
                    t_v = em.run("vector", lambda v, pq=pq, tt=tt, dcol=dcol: v.tensor_copy(
                        out=r_sb[:, tt * D + dcol: tt * D + dcol + WSL], in_=pq[:, 0:WSL]), sem="vE")
                em.wait("tensor", "vE", t_v)
            tap("ctxT", ctxT_hbm.ap(), [D, TOK], BF16, eng="gpsimd",
                waits=[("gS", ctx_done)])
            matmul_sweep(ctxT_hbm.ap(), bo_sb, "o", o_evac, ctx_done, bo_cast_v)
            tap("osum", r_sb[:, 0:NTT * D], [128, NTT * D], F32, waits=[("vE", em.cur("vE"))])

            scr_free = None
            for tt in range(NTT):
                em.wait("sync", "vE", scr_free)
                t_l = em.dma("sync", scr[:], x_hbm[tt * 128:(tt + 1) * 128, :], "dO")
                em.wait("vector", "dO", t_l)
                t_add = em.run("vector", lambda v, tt=tt: v.tensor_tensor(
                    out=r_sb[:, tt * D:(tt + 1) * D], in0=r_sb[:, tt * D:(tt + 1) * D],
                    in1=scr[:], op=ALU.add), sem="vE")
                em.vd()
                em.run("vector", lambda v, tt=tt: v.tensor_reduce(
                    out=mstat[:, tt:tt + 1], in_=r_sb[:, tt * D:(tt + 1) * D], axis=AXX, op=ALU.add))
                em.wait("scalar", "vE", t_add)
                t_s = em.run("scalar", lambda s, tt=tt: s.activation(
                    out=scrb[:], in_=r_sb[:, tt * D:(tt + 1) * D], func=ACT.Square,
                    accum_out=rstat[:, tt:tt + 1]), sem="sE")
                em.wait("vector", "sE", t_s)
                em.vd()
                em.run("vector", lambda v, tt=tt: v.tensor_scalar(
                    out=mstat[:, tt:tt + 1], in0=mstat[:, tt:tt + 1], scalar1=1.0 / D,
                    scalar2=None, op0=ALU.mult))
                em.vd()
                em.run("vector", lambda v, tt=tt: v.tensor_tensor(
                    out=amx2[:], in0=mstat[:, tt:tt + 1], in1=mstat[:, tt:tt + 1], op=ALU.mult))
                em.run("vector", lambda v, tt=tt: v.tensor_scalar(
                    out=rstat[:, tt:tt + 1], in0=rstat[:, tt:tt + 1], scalar1=1.0 / D,
                    scalar2=None, op0=ALU.mult))
                em.vd()
                em.run("vector", lambda v, tt=tt: v.tensor_tensor(
                    out=rstat[:, tt:tt + 1], in0=rstat[:, tt:tt + 1], in1=amx2[:], op=ALU.subtract))
                em.vd()
                t_v = em.run("vector", lambda v, tt=tt: v.tensor_scalar(
                    out=rstat[:, tt:tt + 1], in0=rstat[:, tt:tt + 1], scalar1=1e-5,
                    scalar2=None, op0=ALU.add), sem="vE")
                em.wait("scalar", "vE", t_v)
                t_s = em.run("scalar", lambda s, tt=tt: s.activation(
                    out=rstat[:, tt:tt + 1], in_=rstat[:, tt:tt + 1], func=ACT.Sqrt), sem="sE")
                em.wait("vector", "sE", t_s)
                em.run("vector", lambda v, tt=tt: v.reciprocal(
                    out=rstat[:, tt:tt + 1], in_=rstat[:, tt:tt + 1]))
                em.vd()
                em.run("vector", lambda v, tt=tt: v.tensor_scalar(
                    out=scr[:], in0=r_sb[:, tt * D:(tt + 1) * D], scalar1=mstat[:, tt:tt + 1],
                    scalar2=None, op0=ALU.subtract))
                scr_free = em.run("vector", lambda v: v.tensor_reduce(
                    out=amx2[:], in_=scr[:], axis=AXX, op=ALU.max, apply_absolute_value=True), sem="vE")
                em.vd()
                em.run("vector", lambda v, tt=tt: v.tensor_tensor(
                    out=amx2[:], in0=amx2[:], in1=rstat[:, tt:tt + 1], op=ALU.mult))
                em.vd()
                if tt == 0:
                    em.run("vector", lambda v: v.tensor_copy(out=amx[:], in_=amx2[:]))
                else:
                    em.run("vector", lambda v: v.tensor_tensor(
                        out=amx[:], in0=amx[:], in1=amx2[:], op=ALU.max))
                em.vd()

            tap("r1", r_sb[:, 0:NTT * D], [128, NTT * D], F32, waits=[("vE", em.cur("vE"))])
            tap("mst", mstat[:], [128, NTT], F32, waits=[("vE", em.cur("vE"))])
            tap("rst", rstat[:], [128, NTT], F32, waits=[("vE", em.cur("vE"))])
            tap("amx", amx[:], [128, 1], F32, waits=[("vE", em.cur("vE"))])
            t_v = em.run("vector", lambda v: v.engine_nop(), sem="vE")
            em.wait("gpsimd", "vE", t_v)
            t_g = em.dma("gpsimd", red_hbm.ap(), amx[:], "gS")
            em.wait("gpsimd", "gS", t_g)
            t_g = em.dma("gpsimd", red_sb[:], red_hbm.ap(), "gS")
            em.wait("vector", "gS", t_g)
            em.run("vector", lambda v: v.tensor_reduce(
                out=one_sc[0:1, 1:2], in_=red_sb[0:1, :], axis=AXX, op=ALU.max))
            em.vd()
            em.run("vector", lambda v: v.tensor_copy(
                out=one_sc[0:1, 0:8], in_=one_sc[0:1, 1:2].to_broadcast([1, 8])))
            t_v = em.run("vector", lambda v: v.engine_nop(), sem="vE")
            em.wait("gpsimd", "vE", t_v)
            t_g = em.dma("gpsimd", ax_in[0:1, 0:8], one_sc[0:1, 0:8], "gS")
            em.wait("gpsimd", "gS", t_g)
            t_cc = em.run("gpsimd", lambda g: g.collective_compute(
                "AllReduce", ALU.max, replica_groups=[list(range(8))],
                ins=[ax_in.ap().opt()], outs=[ax_out.ap().opt()]), sem="cc")
            em.wait("gpsimd", "cc", t_cc)
            t_g = em.dma("gpsimd", sxi2[:], ax_out[0:1, 0:1].to_broadcast([128, 1]), "gS")
            em.wait("vector", "gS", t_g)
            em.run("vector", lambda v: v.tensor_scalar(
                out=sxi2[:], in0=sxi2[:], scalar1=1e-12, scalar2=1.0 / 448.0,
                op0=ALU.max, op1=ALU.mult))
            em.vd()
            em.run("vector", lambda v: v.tensor_tensor(
                out=dq_sc[:], in0=sxi2[:], in1=sw_sc[:], op=ALU.mult))
            em.vd()
            em.run("vector", lambda v: v.tensor_scalar(
                out=dq_sc[:], in0=dq_sc[:], scalar1=4.0, scalar2=None, op0=ALU.mult))
            em.run("vector", lambda v: v.tensor_scalar(
                out=sxi2[:], in0=sxi2[:], scalar1=2.0, scalar2=None, op0=ALU.mult))
            em.vd()
            em.run("vector", lambda v: v.reciprocal(out=sxi2[:], in_=sxi2[:]))
            em.run("vector", lambda v: v.reciprocal(out=amx2[:], in_=dq_sc[:]))
            em.vd()
            bfs_v = em.run("vector", lambda v: v.tensor_scalar(
                out=bfs_sb[:], in0=bfr_sb[:], scalar1=amx2[0:1, 0:1], scalar2=None,
                op0=ALU.mult), sem="vE")

            with ExitStack() as stq:
                ptr2 = stq.enter_context(nc.psum_tensor("ptr2", [128, 1024], BF16))
                for tt in range(NTT):
                    em.wait("vector", "gS", em.cur("gS"))
                    em.run("vector", lambda v, tt=tt: v.tensor_scalar(
                        out=scr[:], in0=r_sb[:, tt * D:(tt + 1) * D], scalar1=mstat[:, tt:tt + 1],
                        scalar2=rstat[:, tt:tt + 1], op0=ALU.subtract, op1=ALU.mult))
                    em.vd()
                    em.run("vector", lambda v, tt=tt: v.tensor_copy(
                        out=r_sb[:, tt * D:(tt + 1) * D], in_=scr[:]))
                    em.run("vector", lambda v: v.tensor_scalar(
                        out=scr8[:], in0=scr[:], scalar1=sxi2[:], scalar2=None, op0=ALU.mult))
                    em.vd()
                    t_c = em.run("vector", lambda v: v.tensor_copy(out=scrb[:], in_=scr8[:]), sem="vE")
                    em.wait("tensor", "vE", t_c)
                    for grp in range(ngrp):
                        for k in range(8):
                            i = grp * 8 + k
                            em.run("tensor", lambda t, i=i, k=k: t.transpose(
                                out=ptr2[:, 128 * k:128 * (k + 1)], in_=scrb[:, 128 * i:128 * (i + 1)],
                                identity=ident[:]))
                        t_t = em.trun_drain(lambda t: t.nop() if hasattr(t, "nop") else t.engine_nop())
                        em.wait("vector", "tE", t_t)
                        t_e = em.run("vector", lambda v, grp=grp: v.tensor_copy(
                            out=scrb[:, 1024 * grp:1024 * (grp + 1)], in_=ptr2[:]), sem="vE")
                        em.wait("tensor", "vE", t_e)
                    em.wait("gpsimd", "vE", em.cur("vE"))
                    for half in range(2):
                        hh = nd // 2
                        em.dma("gpsimd",
                               qxT_hbm.ap()[:, tt * 128:(tt + 1) * 128]
                               .rearrange("(c p) w -> p c w", p=128)
                               [:, hh * half:hh * (half + 1)],
                               scrb[:, :].rearrange("p (c w) -> p c w", w=128)
                               [:, hh * half:hh * (half + 1)], "gS")

            tap("sxdq", dq_sc[:], [128, 1], F32, waits=[("vE", em.cur("vE"))])
            tap("y0", r_sb[:, 0:NTT * D], [128, NTT * D], F32, waits=[("vE", em.cur("vE"))])
            def f_evac(tt, pair, pa, pb, t_mm):
                em.wait("vector", "tE", t_mm)
                t_v = None
                for which, pq in ((0, pa), (1, pb)):
                    dcol = (2 * pair + which) * WSL
                    t_v = em.run("vector", lambda v, pq=pq, tt=tt, dcol=dcol: v.scalar_tensor_tensor(
                        out=r_sb[:, tt * D + dcol: tt * D + dcol + WSL], in0=pq[:, 0:WSL],
                        scalar=dq_sc[:], in1=r_sb[:, tt * D + dcol: tt * D + dcol + WSL],
                        op0=ALU.mult, op1=ALU.add), sem="vE")
                em.wait("tensor", "vE", t_v)
            matmul_sweep(qxT_hbm.ap(), bfs_sb, "f", f_evac, em.cur("gS"), bfs_v)

            tap("r2", r_sb[:, 0:NTT * D], [128, NTT * D], F32, waits=[("vE", em.cur("vE"))])
            for tt in range(NTT):
                em.run("vector", lambda v, tt=tt: v.tensor_reduce(
                    out=mstat[:, tt:tt + 1], in_=r_sb[:, tt * D:(tt + 1) * D], axis=AXX, op=ALU.add))
                t_v = em.run("vector", lambda v: v.engine_nop(), sem="vE")
                em.wait("scalar", "vE", t_v)
                t_s = em.run("scalar", lambda s, tt=tt: s.activation(
                    out=scrb[:], in_=r_sb[:, tt * D:(tt + 1) * D], func=ACT.Square,
                    accum_out=rstat[:, tt:tt + 1]), sem="sE")
                em.wait("vector", "sE", t_s)
                em.vd()
                em.run("vector", lambda v, tt=tt: v.tensor_scalar(
                    out=mstat[:, tt:tt + 1], in0=mstat[:, tt:tt + 1], scalar1=1.0 / D,
                    scalar2=None, op0=ALU.mult))
                em.vd()
                em.run("vector", lambda v, tt=tt: v.tensor_tensor(
                    out=amx2[:], in0=mstat[:, tt:tt + 1], in1=mstat[:, tt:tt + 1], op=ALU.mult))
                em.run("vector", lambda v, tt=tt: v.tensor_scalar(
                    out=rstat[:, tt:tt + 1], in0=rstat[:, tt:tt + 1], scalar1=1.0 / D,
                    scalar2=None, op0=ALU.mult))
                em.vd()
                em.run("vector", lambda v, tt=tt: v.tensor_tensor(
                    out=rstat[:, tt:tt + 1], in0=rstat[:, tt:tt + 1], in1=amx2[:], op=ALU.subtract))
                em.vd()
                t_v = em.run("vector", lambda v, tt=tt: v.tensor_scalar(
                    out=rstat[:, tt:tt + 1], in0=rstat[:, tt:tt + 1], scalar1=1e-5,
                    scalar2=None, op0=ALU.add), sem="vE")
                em.wait("scalar", "vE", t_v)
                t_s = em.run("scalar", lambda s, tt=tt: s.activation(
                    out=rstat[:, tt:tt + 1], in_=rstat[:, tt:tt + 1], func=ACT.Sqrt), sem="sE")
                em.wait("vector", "sE", t_s)
                em.run("vector", lambda v, tt=tt: v.reciprocal(
                    out=rstat[:, tt:tt + 1], in_=rstat[:, tt:tt + 1]))
                em.vd()
                em.wait("vector", "dOut", em.cur("dOut"))
                t_v = em.run("vector", lambda v, tt=tt: v.tensor_scalar(
                    out=scr[:], in0=r_sb[:, tt * D:(tt + 1) * D], scalar1=mstat[:, tt:tt + 1],
                    scalar2=rstat[:, tt:tt + 1], op0=ALU.subtract, op1=ALU.mult), sem="vE")
                em.wait("sync", "vE", t_v)
                em.dma("sync", out_hbm[tt * 128:(tt + 1) * 128, :], scr[:], "dOut")

            em.wait("sync", "dOut", em.cur("dOut"))

        with nc.Block() as block:
            @block.sync
            def _(e):
                for fn in em.ops["sync"]:
                    fn(e)

            @block.gpsimd
            def _(e):
                for fn in em.ops["gpsimd"]:
                    fn(e)

            @block.vector
            def _(e):
                for fn in em.ops["vector"]:
                    fn(e)

            @block.scalar
            def _(e):
                for fn in em.ops["scalar"]:
                    fn(e)

            @block.tensor
            def _(e):
                for fn in em.ops["tensor"]:
                    fn(e)

    return nc


_NC_CACHE = None


def kernel(L0_attn_hidden_states, L0_attn_wq, L0_attn_bq, L0_attn_wk, L0_attn_bk,
           L0_attn_wv, L0_attn_bv, L0_attn_wo, L0_attn_bo, L0_ffn_W, L0_ffn_b):
    global _NC_CACHE
    x = np.ascontiguousarray(np.asarray(L0_attn_hidden_states, dtype=np.float32)).reshape(B * SEQ, D)
    ws = {"q": np.asarray(L0_attn_wq, np.float32), "k": np.asarray(L0_attn_wk, np.float32),
          "v": np.asarray(L0_attn_wv, np.float32), "o": np.asarray(L0_attn_wo, np.float32),
          "f": np.asarray(L0_ffn_W, np.float32)}
    bq = np.ascontiguousarray(np.asarray(L0_attn_bq, np.float32).reshape(D, 1))
    bk = np.ascontiguousarray(np.asarray(L0_attn_bk, np.float32).reshape(D, 1))
    bv = np.ascontiguousarray(np.asarray(L0_attn_bv, np.float32).reshape(D, 1))
    bo = np.ascontiguousarray(np.asarray(L0_attn_bo, np.float32).reshape(1, D))
    bf = np.ascontiguousarray(np.asarray(L0_ffn_b, np.float32).reshape(1, D))

    if _NC_CACHE is None:
        _NC_CACHE = build_nc()
    nc = _NC_CACHE

    in_maps = []
    for c in range(NCORES):
        m = {"x_slab": np.ascontiguousarray(x[c * TOK:(c + 1) * TOK]),
             "bq_col": bq, "bk_col": bk, "bv_col": bv, "bo_row": bo, "bf_row": bf}
        for k, w in ws.items():
            m[f"w{k}_sl"] = np.ascontiguousarray(w[c * WSL:(c + 1) * WSL])
        in_maps.append(m)

    res = run_bass_kernel_spmd(nc, in_maps, core_ids=list(range(NCORES)))
    out = np.concatenate([np.asarray(r["out_slab"]) for r in res.results], axis=0)
    return np.ascontiguousarray(out.reshape(B, SEQ, D).astype(np.float32))


# version 19 (final)
# speedup vs baseline: 1.0278x; 1.0278x over previous
import math
from contextlib import ExitStack

import numpy as np

import concourse.bass as bass
import concourse.mybir as mybir
from concourse.bass_utils import run_bass_kernel_spmd

F32 = mybir.dt.float32
BF16 = mybir.dt.bfloat16
FP8 = mybir.dt.float8e4
ALU = mybir.AluOpType
ACT = mybir.ActivationFunctionType
AXX = mybir.AxisListType.X

NCORES = 8
B = 2
SEQ = 4096
D = 4096
NH = 32
DH = 128
TOK = 1024
NTT = TOK // 128
KTOP = 819
ISC = 1.0 / math.sqrt(DH)
NPROBE = 5
G = 4
WSL = D // NCORES
DBG = False


class Emit:
    ENGINES = ("sync", "gpsimd", "vector", "scalar", "tensor")

    def __init__(self, nc):
        self.nc = nc
        self.ops = {e: [] for e in self.ENGINES}
        self.ctr = {}
        self.sems = {}

    def on(self, eng, fn):
        self.ops[eng].append(fn)

    def cur(self, sem):
        return self.ctr.get(sem, 0)

    def dma(self, eng, out, in_, sem):
        s = self.sems[sem]
        self.ctr[sem] = self.cur(sem) + 16
        tgt = self.ctr[sem]
        self.on(eng, lambda e, out=out, in_=in_, s=s: e.dma_start(out=out, in_=in_).then_inc(s, 16))
        return tgt

    def wait(self, eng, sem, tgt):
        if not tgt:
            return
        s = self.sems[sem]
        self.on(eng, lambda e, s=s, tgt=tgt: e.wait_ge(s, tgt))

    def vd(self):
        self.on("vector", lambda e: e.drain())

    def trun_drain(self, fn, sem="tE"):
        s = self.sems[sem]
        self.ctr[sem] = self.cur(sem) + 1
        tgt = self.ctr[sem]
        def wrapped(e, fn=fn, s=s):
            fn(e)
            e.drain().then_inc(s, 1)
        self.on("tensor", wrapped)
        return tgt

    def barrier(self, eng, sems=("tE", "vE", "sE", "gS")):
        for s in sems:
            self.wait(eng, s, self.cur(s))

    def run(self, eng, fn, sem=None):
        if sem is None:
            self.on(eng, fn)
            return None
        s = self.sems[sem]
        self.ctr[sem] = self.cur(sem) + 1
        tgt = self.ctr[sem]
        if eng in ("vector", "scalar"):
            def wrapped(e, fn=fn, s=s):
                fn(e)
                e.drain().then_inc(s, 1)
            self.on(eng, wrapped)
        else:
            self.on(eng, lambda e, fn=fn, s=s: fn(e).then_inc(s, 1))
        return tgt


def build_nc():
    nd = D // 128
    ngrp = nd // 8
    nkc = SEQ // 512
    nkv = SEQ // 128
    ndo = D // 512
    nc = bass.Bass("TRN2", target_bir_lowering=False, detect_race_conditions=False)

    x_hbm = nc.dram_tensor("x_slab", [TOK, D], F32, kind="ExternalInput")
    w_sl = {m: nc.dram_tensor(f"w{m}_sl", [WSL, D], F32, kind="ExternalInput")
            for m in ("q", "k", "v", "o", "f")}
    b_col = {m: nc.dram_tensor(f"b{m}_col", [D, 1], F32, kind="ExternalInput")
             for m in ("q", "k", "v")}
    bo_row = nc.dram_tensor("bo_row", [1, D], F32, kind="ExternalInput")
    bf_row = nc.dram_tensor("bf_row", [1, D], F32, kind="ExternalInput")
    out_hbm = nc.dram_tensor("out_slab", [TOK, D], F32, kind="ExternalOutput")

    wT_in = {m: nc.dram_tensor(f"w{m}T_in", [D, WSL], BF16) for m in ("q", "k", "v", "o", "f")}
    wT_out = {m: nc.dram_tensor(f"w{m}T_out", [NCORES * D, WSL], BF16) for m in ("q", "k", "v", "o", "f")}
    kT_in = nc.dram_tensor("kT_in", [D, TOK], BF16)
    kT_out = nc.dram_tensor("kT_out", [4 * D, TOK], BF16)
    v_in = nc.dram_tensor("v_in", [D, TOK], BF16)
    v_out = nc.dram_tensor("v_out", [4 * D, TOK], BF16)
    ctxT_hbm = nc.dram_tensor("ctxT_hbm", [D, TOK], BF16)
    qxT_hbm = nc.dram_tensor("qxT_hbm", [D, TOK], BF16)
    red_hbm = nc.dram_tensor("red_hbm", [1, 128], F32)
    am_in = nc.dram_tensor("am_in", [1, 8], F32)
    am_out = nc.dram_tensor("am_out", [1, 8], F32)
    ax_in = nc.dram_tensor("ax_in", [1, 8], F32)
    ax_out = nc.dram_tensor("ax_out", [1, 8], F32)

    em = Emit(nc)

    dbg = {}

    def tap(name, ap, shape, dtype, eng="sync", waits=()):
        if not DBG:
            return
        t = nc.dram_tensor(f"dbg_{name}", shape, dtype, kind="ExternalOutput")
        dbg[name] = t
        for sem, tgt in waits:
            em.wait(eng, sem, tgt)
        em.dma(eng, t.ap(), ap, "dOut")

    with ExitStack() as st:
        ec = st.enter_context
        for name in ("dW", "dA", "dB", "dK", "dV", "dO", "dOut", "gS", "cc",
                     "tE", "vE", "sE", "gE"):
            em.sems[name] = ec(nc.semaphore(name))

        ident = ec(nc.sbuf_tensor("ident", [128, 128], BF16))
        iota_r = ec(nc.sbuf_tensor("iota_r", [128, 128], F32))
        iota_c = ec(nc.sbuf_tensor("iota_c", [128, 128], F32))
        ones_row = ec(nc.sbuf_tensor("ones_row", [1, 128], BF16))
        bqkv = {m: ec(nc.sbuf_tensor(f"b{m}_sb", [128, NH], F32)) for m in ("q", "k", "v")}
        bo_sb = ec(nc.sbuf_tensor("bo_sb", [1, D], BF16))
        bfr_sb = ec(nc.sbuf_tensor("bfr_sb", [1, D], F32))
        bfs_sb = ec(nc.sbuf_tensor("bfs_sb", [1, D], BF16))
        sw_sc = ec(nc.sbuf_tensor("sw_sc", [128, 1], F32))
        swi2 = ec(nc.sbuf_tensor("swi2", [128, 1], F32))
        sxi2 = ec(nc.sbuf_tensor("sxi2", [128, 1], F32))
        dq_sc = ec(nc.sbuf_tensor("dq_sc", [128, 1], F32))
        red_sb = ec(nc.sbuf_tensor("red_sb", [1, 128], F32))
        one_sc = ec(nc.sbuf_tensor("one_sc", [1, 8], F32))

        em.run("gpsimd", lambda g: g.iota(iota_r[:], pattern=[[0, 128]], base=0,
                                          channel_multiplier=1, allow_small_or_imprecise_dtypes=True))
        t_iota = em.run("gpsimd", lambda g: g.iota(iota_c[:], pattern=[[1, 128]], base=0,
                                                   channel_multiplier=0,
                                                   allow_small_or_imprecise_dtypes=True), sem="gE")
        em.wait("vector", "gE", t_iota)
        em.run("vector", lambda v: v.tensor_tensor(out=ident[:], in0=iota_r[:], in1=iota_c[:],
                                                   op=ALU.is_equal))
        em.run("vector", lambda v: v.memset(ones_row[:], 1.0))

        for m in ("q", "k", "v"):
            for h in range(NH):
                em.dma("sync", bqkv[m][:, h:h + 1], b_col[m][128 * h:128 * (h + 1), 0:1], "dB")
        t_bias = em.dma("sync", bfr_sb[:], bf_row.ap(), "dB")
        em.wait("scalar", "dB", t_bias)
        em.wait("vector", "dB", t_bias)

        cc_tgt = {}
        bo_cast_v = None
        with ExitStack() as stW:
            ecW = stW.enter_context
            wrow = [ecW(nc.sbuf_tensor(f"wrow{i}", [128, D], F32)) for i in range(2)]
            wbf = ecW(nc.sbuf_tensor("wbf", [128, D], BF16))
            wq8 = ecW(nc.sbuf_tensor("wq8", [128, D], FP8))
            wTsb = ecW(nc.sbuf_tensor("wTsb", [128, 32 * WSL], BF16))
            pW = [ecW(nc.psum_tensor("pW%d" % i, [128, 1024], BF16)) for i in range(2)]
            amw = ecW(nc.sbuf_tensor("amw", [128, 1], F32))
            amw2 = ecW(nc.sbuf_tensor("amw2", [128, 1], F32))
            bo_f32 = ecW(nc.sbuf_tensor("bo_f32", [1, D], F32))

            t_bo = em.dma("sync", bo_f32[:], bo_row.ap(), "dB")
            em.wait("vector", "dB", t_bo)
            bo_cast_v = em.run("vector", lambda v: v.tensor_copy(out=bo_sb[:], in_=bo_f32[:]), sem="vE")

            wrow_free = [None, None]

            def load_wrow(m, j):
                buf = wrow[j % 2]
                em.wait("sync", "vE", wrow_free[j % 2])
                t_l = em.dma("sync", buf[:], w_sl[m][128 * j:128 * (j + 1), :], "dW")
                em.wait("vector", "dW", t_l)
                return buf

            def preprocess_weight(m, quant):
                if quant:
                    t_c = None
                    for j in range(WSL // 128):
                        buf = load_wrow(m, j)
                        if j == 0:
                            t_c = em.run("vector", lambda v, b=buf: v.tensor_reduce(
                                out=amw[:], in_=b[:], axis=AXX, op=ALU.max,
                                apply_absolute_value=True), sem="vE")
                        else:
                            em.run("vector", lambda v, b=buf: v.tensor_reduce(
                                out=amw2[:], in_=b[:], axis=AXX, op=ALU.max,
                                apply_absolute_value=True))
                            em.vd()
                            t_c = em.run("vector", lambda v: v.tensor_tensor(
                                out=amw[:], in0=amw[:], in1=amw2[:], op=ALU.max), sem="vE")
                            em.vd()
                        wrow_free[j % 2] = t_c
                    em.wait("gpsimd", "vE", t_c)
                    t_g = em.dma("gpsimd", red_hbm.ap(), amw[:], "gS")
                    em.wait("gpsimd", "gS", t_g)
                    t_g = em.dma("gpsimd", red_sb[:], red_hbm.ap(), "gS")
                    em.wait("vector", "gS", t_g)
                    em.run("vector", lambda v: v.tensor_reduce(
                        out=one_sc[0:1, 0:1], in_=red_sb[0:1, :], axis=AXX, op=ALU.max))
                    em.vd()
                    em.run("vector", lambda v: v.tensor_copy(
                        out=one_sc[0:1, 0:8], in_=one_sc[0:1, 0:1].to_broadcast([1, 8])))
                    t_v = em.run("vector", lambda v: v.engine_nop(), sem="vE")
                    em.wait("gpsimd", "vE", t_v)
                    t_g = em.dma("gpsimd", am_in[0:1, 0:8], one_sc[0:1, 0:8], "gS")
                    em.wait("gpsimd", "gS", t_g)
                    t_cc = em.run("gpsimd", lambda g: g.collective_compute(
                        "AllReduce", ALU.max, replica_groups=[list(range(8))],
                        ins=[am_in.ap().opt()], outs=[am_out.ap().opt()]), sem="cc")
                    em.wait("gpsimd", "cc", t_cc)
                    t_g = em.dma("gpsimd", sw_sc[:], am_out[0:1, 0:1].to_broadcast([128, 1]), "gS")
                    em.wait("vector", "gS", t_g)
                    em.run("vector", lambda v: v.tensor_scalar(
                        out=sw_sc[:], in0=sw_sc[:], scalar1=1e-12, scalar2=1.0 / 448.0,
                        op0=ALU.max, op1=ALU.mult))
                    em.vd()
                    em.run("vector", lambda v: v.tensor_scalar(
                        out=swi2[:], in0=sw_sc[:], scalar1=2.0, scalar2=None, op0=ALU.mult))
                    em.vd()
                    em.run("vector", lambda v: v.reciprocal(out=swi2[:], in_=swi2[:]))
                    em.vd()

                ag_guard = em.cur("gS")
                for j in range(WSL // 128):
                    buf = load_wrow(m, j)
                    if quant:
                        em.run("vector", lambda v, b=buf: v.tensor_scalar(
                            out=wq8[:], in0=b[:], scalar1=swi2[:], scalar2=None, op0=ALU.mult))
                        em.vd()
                        t_c = em.run("vector", lambda v: v.tensor_copy(out=wbf[:], in_=wq8[:]), sem="vE")
                    else:
                        t_c = em.run("vector", lambda v, b=buf: v.tensor_copy(out=wbf[:], in_=b[:]), sem="vE")
                    wrow_free[j % 2] = t_c
                    em.wait("tensor", "vE", t_c)
                    if j == 0:
                        em.wait("vector", "gS", ag_guard)
                    for grp in range(ngrp):
                        pt = pW[grp % 2]
                        for k in range(8):
                            i = grp * 8 + k
                            em.run("tensor", lambda t, i=i, k=k, pt=pt: t.transpose(
                                out=pt[:, 128 * k:128 * (k + 1)], in_=wbf[:, 128 * i:128 * (i + 1)],
                                identity=ident[:]))
                        t_t = em.trun_drain(lambda t: t.nop() if hasattr(t, "nop") else t.engine_nop())
                        em.wait("vector", "tE", t_t)
                        t_e = em.run("vector", lambda v, pt=pt, grp=grp, j=j: v.tensor_copy(
                            out=wTsb[:, :].rearrange("p (c w) -> p c w", w=WSL)
                            [:, grp * 8:grp * 8 + 8, 128 * j:128 * (j + 1)],
                            in_=pt[:].rearrange("p (c w) -> p c w", w=128)), sem="vE")
                        em.wait("tensor", "vE", t_e)
                t_v = em.run("vector", lambda v: v.engine_nop(), sem="vE")
                em.wait("gpsimd", "vE", t_v)
                t_g = None
                for half in range(2):
                    hh = nd // 2
                    t_g = em.dma("gpsimd",
                                 wT_in[m].ap().rearrange("(c p) w -> p c w", p=128)
                                 [:, hh * half:hh * (half + 1)],
                                 wTsb[:, :].rearrange("p (c w) -> p c w", w=WSL)
                                 [:, hh * half:hh * (half + 1)], "gS")
                em.wait("gpsimd", "gS", t_g)
                t_ag = em.run("gpsimd", lambda g, m=m: g.collective_compute(
                    "AllGather", ALU.bypass, replica_groups=[list(range(8))],
                    ins=[wT_in[m].ap().opt()], outs=[wT_out[m].ap().opt()]), sem="cc")
                em.wait("gpsimd", "cc", t_ag)
                return t_ag

            for m in ("q", "k", "v", "o"):
                cc_tgt[m] = preprocess_weight(m, False)
            cc_tgt["f"] = preprocess_weight("f", True)

        st01 = ExitStack()
        qT_sb = st01.enter_context(nc.sbuf_tensor("qT_sb", [128, NH * TOK], BF16))
        with ExitStack() as st0:
            ec0 = st0.enter_context
            xrow = [ec0(nc.sbuf_tensor("xrow0", [128, D], F32))]
            xbf = ec0(nc.sbuf_tensor("xbf", [128, D], BF16))
            xT_sb = ec0(nc.sbuf_tensor("xT_sb", [128, 32 * TOK], BF16))
            p0 = [ec0(nc.psum_tensor("p0_%d" % i, [128, 1024], BF16)) for i in range(2)]
            wstrip = [ec0(nc.sbuf_tensor(f"wstrip{i}", [128, D], BF16)) for i in range(2)]
            pp = [ec0(nc.psum_tensor("pp%d" % i, [128, 512], F32)) for i in range(2)]
            kvrow = ec0(nc.sbuf_tensor("kvrow", [128, 1024], BF16))

            em.barrier("sync")
            em.barrier("gpsimd")
            xrow_free = [None]
            for j in range(NTT):
                buf = xrow[0]
                em.wait("sync", "vE", xrow_free[0])
                t_l = em.dma("sync", buf[:], x_hbm[128 * j:128 * (j + 1), :], "dA")
                em.wait("vector", "dA", t_l)
                t_c = em.run("vector", lambda v, b=buf: v.tensor_copy(out=xbf[:], in_=b[:]), sem="vE")
                xrow_free[0] = t_c
                em.wait("tensor", "vE", t_c)
                for grp in range(ngrp):
                    pt = p0[grp % 2]
                    for k in range(8):
                        i = grp * 8 + k
                        em.run("tensor", lambda t, i=i, k=k, pt=pt: t.transpose(
                            out=pt[:, 128 * k:128 * (k + 1)], in_=xbf[:, 128 * i:128 * (i + 1)],
                            identity=ident[:]))
                    t_t = em.trun_drain(lambda t: t.nop() if hasattr(t, "nop") else t.engine_nop())
                    em.wait("vector", "tE", t_t)
                    t_e = em.run("vector", lambda v, pt=pt, grp=grp, j=j: v.tensor_copy(
                        out=xT_sb[:, :].rearrange("p (c w) -> p c w", w=TOK)
                        [:, grp * 8:grp * 8 + 8, 128 * j:128 * (j + 1)],
                        in_=pt[:].rearrange("p (c w) -> p c w", w=128)), sem="vE")
                    em.wait("tensor", "vE", t_e)

            tap("xT", xT_sb[:, 0:512], [128, 512], BF16, waits=[("vE", em.cur("vE"))])
            wstrip_free = [None, None]

            def proj(m, sink):
                em.wait("sync", "cc", cc_tgt[m])
                for c in range(nd):
                    blk, col = (c * 128) // WSL, (c * 128) % WSL
                    sb = wstrip[c % 2]
                    em.wait("sync", "tE", wstrip_free[c % 2])
                    t_l = None
                    for qtr in range(2):
                        hh = nd // 2
                        t_l = em.dma("sync",
                                     sb[:, :].rearrange("p (ci w) -> p ci w", w=128)
                                     [:, hh * qtr:hh * (qtr + 1)],
                                     wT_out[m].ap()[blk * D:(blk + 1) * D, col:col + 128]
                                     .rearrange("(ci p) w -> p ci w", p=128)
                                     [:, hh * qtr:hh * (qtr + 1)], "dA")
                    em.wait("tensor", "dA", t_l)
                    for half in range(TOK // 512 if TOK >= 512 else 1):
                        psum = pp[half]
                        hw = min(512, TOK)
                        t_mm = None
                        for di in range(nd):
                            t_mm = em.run("tensor", lambda t, sb=sb, di=di, half=half, psum=psum, hw=hw: t.matmul(
                                out=psum[:, 0:hw], lhsT=sb[:, 128 * di:128 * (di + 1)],
                                rhs=xT_sb[:, di * TOK + hw * half: di * TOK + hw * half + hw],
                                start=(di == 0), stop=(di == nd - 1)), sem="tE")
                        wstrip_free[c % 2] = t_mm
                        sink(c, half, psum, t_mm)

            def q_sink(c, half, psum, t_mm):
                em.wait("vector", "tE", t_mm)
                hw = min(512, TOK)
                t_s = em.run("vector", lambda v, psum=psum, c=c, half=half, hw=hw: v.tensor_scalar(
                    out=qT_sb[:, c * TOK + hw * half: c * TOK + hw * (half + 1)],
                    in0=psum[:, 0:hw], scalar1=bqkv["q"][:, c:c + 1], scalar2=None, op0=ALU.add), sem="vE")
                em.wait("tensor", "vE", t_s)
            proj("q", q_sink)
            tap("wstrip", wstrip[1][:, 0:512], [128, 512], BF16, waits=[("tE", em.cur("tE"))])
            tap("qT", qT_sb[:, 0:512], [128, 512], BF16, waits=[("vE", em.cur("vE"))])

            def k_sink(c, half, psum, t_mm):
                em.wait("vector", "tE", t_mm)
                em.wait("vector", "gS", em.cur("gS"))
                hw = min(512, TOK)
                t_s = em.run("vector", lambda v, psum=psum, c=c, hw=hw: v.tensor_scalar(
                    out=kvrow[:, 0:hw], in0=psum[:, 0:hw], scalar1=bqkv["k"][:, c:c + 1],
                    scalar2=None, op0=ALU.add), sem="vE")
                em.wait("tensor", "vE", t_s)
                em.wait("gpsimd", "vE", t_s)
                em.dma("gpsimd", kT_in.ap()[c * 128:(c + 1) * 128, hw * half:hw * (half + 1)],
                       kvrow[:, 0:hw], "gS")
            proj("k", k_sink)

            def v_sink(c, half, psum, t_mm):
                em.wait("vector", "tE", t_mm)
                em.wait("vector", "gS", em.cur("gS"))
                hw = min(512, TOK)
                t_s = em.run("vector", lambda v, psum=psum, hw=hw: v.tensor_scalar(
                    out=kvrow[:, 0:hw], in0=psum[:, 0:hw], scalar1=0.0, scalar2=None,
                    op0=ALU.add), sem="vE")
                em.wait("gpsimd", "vE", t_s)
                em.dma("gpsimd", v_in.ap()[c * 128:(c + 1) * 128, hw * half:hw * (half + 1)],
                       kvrow[:, 0:hw], "gS")
            proj("v", v_sink)
            em.wait("gpsimd", "gS", em.cur("gS"))
            cc_k = em.run("gpsimd", lambda g: g.collective_compute(
                "AllGather", ALU.bypass, replica_groups=[[0, 1, 2, 3], [4, 5, 6, 7]],
                ins=[kT_in.ap().opt()], outs=[kT_out.ap().opt()]), sem="cc")
            tap("vin", v_in.ap(), [D, TOK], BF16, eng="gpsimd", waits=[("gS", em.cur("gS"))])
            cc_v = em.run("gpsimd", lambda g: g.collective_compute(
                "AllGather", ALU.bypass, replica_groups=[[0, 1, 2, 3], [4, 5, 6, 7]],
                ins=[v_in.ap().opt()], outs=[v_out.ap().opt()]), sem="cc")
            em.wait("gpsimd", "cc", cc_v)

        gG = min(G, NTT)
        with ExitStack() as st1:
            ec1 = st1.enter_context
            kT_h = ec1(nc.sbuf_tensor("kT_h", [128, SEQ], BF16))
            v_h = ec1(nc.sbuf_tensor("v_h", [128, SEQ], BF16))
            s_g = [ec1(nc.sbuf_tensor(f"s_g{i}", [128, SEQ], F32)) for i in range(min(G, NTT))]
            es2 = [ec1(nc.sbuf_tensor(f"es2_{i}", [128, SEQ], BF16)) for i in range(2)]
            zpT = ec1(nc.sbuf_tensor("zpT", [128, SEQ], BF16))
            ctx_t = [ec1(nc.sbuf_tensor(f"ctx_t{i}", [128, 128], BF16)) for i in range(2)]
            qcols = 512 * max(1, (SEQ // 8) // 512)
            sqd = ec1(nc.sbuf_tensor("sqd", [128, qcols], BF16))
            ksum = ec1(nc.sbuf_tensor("ksum", [128, 2], F32))
            musq = ec1(nc.sbuf_tensor("musq", [128, 2 * gG], F32))
            ksumb = ec1(nc.sbuf_tensor("ksumb", [128, 512], BF16))
            stt = {n: ec1(nc.sbuf_tensor(f"st_{n}", [128, gG],
                                         mybir.dt.int32 if n in ("ge", "lt") else F32))
                   for n in ("lo", "hi", "mid", "cnt", "ge", "lt", "mu", "sd", "sum", "s2", "mq")}
            ntg = ec1(nc.sbuf_tensor("ntg", [128, gG], F32))
            zrow = ec1(nc.sbuf_tensor("zrow", [128, gG], F32))
            rz = ec1(nc.sbuf_tensor("rz", [128, 1], F32))
            lo_all = ec1(nc.sbuf_tensor("lo_all", [128, NH * NTT], F32)) if DBG else None
            z_all = ec1(nc.sbuf_tensor("z_all", [128, NH * NTT], F32)) if DBG else None
            p_s = [ec1(nc.psum_tensor("p1s%d" % i, [128, 512], F32)) for i in range(4)]
            p_t2 = [ec1(nc.psum_tensor("p1t%d" % i, [128, 1024], BF16)) for i in range(2)]
            p_o = [ec1(nc.psum_tensor("p1o%d" % i, [128, 128], F32)) for i in range(2)]

            em.wait("sync", "cc", cc_v)
            em.barrier("sync")
            em.barrier("gpsimd")
            em.run("vector", lambda v: v.memset(ksumb[:], 0.0))
            em.vd()
            tap("vout", v_out.ap(), [4 * D, TOK], BF16, eng="gpsimd", waits=[("cc", cc_v)])

            npk = min(8, nkv)
            ngv = max(1, nkv // 8)
            ngq = NTT // gG
            st_p1 = {"kq_done": None, "t_k": None, "t_v": None}
            tr_prev = [None, None]
            ctx_dma = [None, None]

            def load_kv(h):
                em.wait("sync", "tE", st_p1["kq_done"])
                t_k = None
                for r in range(4):
                    t_k = em.dma("sync", kT_h[:, TOK * r:TOK * (r + 1)],
                                 kT_out.ap()[r * D + h * 128: r * D + (h + 1) * 128, :], "dK")
                st_p1["t_k"] = t_k
                em.wait("sync", "tE", tr_prev[0])
                t_v = None
                for r in range(4):
                    t_v = em.dma("sync", es2[0][:, TOK * r:TOK * (r + 1)],
                                 v_out.ap()[r * D + h * 128: r * D + (h + 1) * 128, :], "dV")
                st_p1["t_v"] = t_v

            def stage_v(h):
                em.wait("tensor", "dV", st_p1["t_v"])
                ev = [None, None]
                for grp in range(ngv):
                    if ev[grp % 2] is not None:
                        em.wait("tensor", "vE", ev[grp % 2])
                    for k in range(npk):
                        kc = grp * npk + k
                        em.run("tensor", lambda t, kc=kc, k=k, grp=grp: t.transpose(
                            out=p_t2[grp % 2][:, 128 * k:128 * (k + 1)],
                            in_=es2[0][:, 128 * kc:128 * (kc + 1)], identity=ident[:]))
                    t_t2 = em.trun_drain(lambda t: t.nop() if hasattr(t, "nop") else t.engine_nop())
                    em.wait("vector", "tE", t_t2)
                    ev[grp % 2] = em.run("vector", lambda v, grp=grp: v.tensor_copy(
                        out=v_h[:, 128 * npk * grp:128 * npk * (grp + 1)],
                        in_=p_t2[grp % 2][:, 0:128 * npk]), sem="vE")
                em.wait("tensor", "vE", ev[(ngv - 1) % 2])
                if ngv > 1:
                    em.wait("tensor", "vE", ev[ngv % 2])
                tr_prev[0] = em.cur("tE")
                if h == 0:
                    tap("kTh", kT_h[:, 0:512], [128, 512], BF16, waits=[("dK", st_p1["t_k"])])
                    tap("vh", v_h[:, 0:512], [128, 512], BF16, waits=[("vE", em.cur("vE"))])

            def emit_scores(h, gq, guards):
                em.wait("tensor", "dK", st_p1["t_k"])
                if gq == 0:
                    em.wait("vector", "dK", st_p1["t_k"])
                    em.run("vector", lambda v: v.tensor_reduce(
                        out=ksum[:, 0:1], in_=kT_h[:, 0:SEQ], axis=AXX, op=ALU.add))
                    em.run("vector", lambda v: v.tensor_reduce(
                        out=ksum[:, 1:2], in_=kT_h[:, 0:qcols], axis=AXX, op=ALU.add))
                    em.vd()
                    t_ks = em.run("vector", lambda v: v.tensor_copy(
                        out=ksumb[:, 0:2], in_=ksum[:]), sem="vE")
                    em.wait("tensor", "vE", t_ks)
                nc4 = min(4, nkc)
                spl = max(1, nc4 // 2)
                for g in range(gG):
                    qt = gq * gG + g
                    kc0_evac = None
                    for rnd in range(max(1, nkc // 4)):
                        if rnd == 0:
                            em.wait("tensor", "vE", st_p1.get("musq_cp"))
                        t_mm = None
                        for kc4 in range(nc4):
                            kc = rnd * nc4 + kc4
                            t_mm = em.run("tensor", lambda t, h=h, qt=qt, kc=kc, kc4=kc4: t.matmul(
                                out=p_s[kc4][:],
                                lhsT=qT_sb[:, h * TOK + 128 * qt: h * TOK + 128 * (qt + 1)],
                                rhs=kT_h[:, 512 * kc:512 * (kc + 1)],
                                start=True, stop=True), sem="tE")
                        em.wait("scalar", "tE", t_mm)
                        em.wait("vector", "tE", t_mm)
                        if rnd == 0 and guards is not None:
                            em.wait("scalar", "vE", guards[0][g])
                            em.wait("vector", "sE", guards[1][g])
                        t_s = None
                        t_v = None
                        for kc4 in range(nc4):
                            kc = rnd * nc4 + kc4
                            if kc4 < spl:
                                t_v = em.run("vector", lambda v, g=g, kc=kc, kc4=kc4: v.tensor_copy(
                                    out=s_g[g][:, 512 * kc:512 * (kc + 1)], in_=p_s[kc4][:]),
                                    sem="vE")
                                if kc == 0:
                                    kc0_evac = t_v
                            else:
                                t_s = em.run("scalar", lambda s, g=g, kc=kc, kc4=kc4: s.activation(
                                    out=s_g[g][:, 512 * kc:512 * (kc + 1)], in_=p_s[kc4][:],
                                    func=ACT.Copy), sem="sE")
                        if t_s is not None:
                            em.wait("tensor", "sE", t_s)
                        em.wait("tensor", "vE", t_v)
                    t_mu = em.run("tensor", lambda t, h=h, qt=qt: t.matmul(
                        out=p_s[0][:],
                        lhsT=qT_sb[:, h * TOK + 128 * qt: h * TOK + 128 * (qt + 1)],
                        rhs=ksumb[:], start=True, stop=True), sem="tE")
                    em.wait("vector", "tE", t_mu)
                    st_p1["musq_cp"] = em.run("vector", lambda v, g=g: v.tensor_copy(
                        out=musq[:, 2 * g:2 * g + 2], in_=p_s[0][:, 0:2]), sem="vE")
                    em.wait("scalar", "vE", kc0_evac)
                    sm = "sE" if g == gG - 1 else None
                    t_sq = em.run("scalar", lambda s, g=g: s.activation(
                        out=sqd[:], in_=s_g[g][:, 0:qcols], func=ACT.Square,
                        accum_out=stt["s2"][:, g:g + 1]), sem=sm)
                st_p1["kq_done"] = em.cur("tE")
                return t_sq

            def emit_stats(t_sq, first):
                em.wait("vector", "sE", t_sq)
                for g in range(gG):
                    em.run("vector", lambda v, g=g: v.tensor_copy(
                        out=stt["sum"][:, g:g + 1], in_=musq[:, 2 * g:2 * g + 1]))
                    em.run("vector", lambda v, g=g: v.tensor_copy(
                        out=stt["mq"][:, g:g + 1], in_=musq[:, 2 * g + 1:2 * g + 2]))
                em.vd()
                if first:
                    tap("s0", s_g[0][:, 0:SEQ], [128, SEQ], F32, waits=[("sE", t_sq)])
                em.run("vector", lambda v: v.tensor_scalar(
                    out=stt["mu"][:], in0=stt["sum"][:], scalar1=1.0 / SEQ, scalar2=None,
                    op0=ALU.mult))
                em.run("vector", lambda v: v.tensor_scalar(
                    out=stt["mq"][:], in0=stt["mq"][:], scalar1=1.0 / qcols, scalar2=None,
                    op0=ALU.mult))
                em.vd()
                em.run("vector", lambda v: v.tensor_tensor(
                    out=stt["mid"][:], in0=stt["mq"][:], in1=stt["mq"][:], op=ALU.mult))
                em.run("vector", lambda v: v.tensor_scalar(
                    out=stt["sd"][:], in0=stt["s2"][:], scalar1=1.0 / qcols, scalar2=None,
                    op0=ALU.mult))
                em.vd()
                em.run("vector", lambda v: v.tensor_tensor(
                    out=stt["sd"][:], in0=stt["sd"][:], in1=stt["mid"][:], op=ALU.subtract))
                em.vd()
                t_v = em.run("vector", lambda v: v.tensor_scalar(
                    out=stt["sd"][:], in0=stt["sd"][:], scalar1=1e-12, scalar2=None,
                    op0=ALU.max), sem="vE")
                em.wait("scalar", "vE", t_v)
                t_s = em.run("scalar", lambda s: s.activation(
                    out=stt["sd"][:], in_=stt["sd"][:], func=ACT.Sqrt), sem="sE")
                em.wait("vector", "sE", t_s)
                em.run("vector", lambda v: v.tensor_scalar(
                    out=stt["lo"][:], in0=stt["sd"][:], scalar1=0.65, scalar2=None, op0=ALU.mult))
                em.run("vector", lambda v: v.tensor_scalar(
                    out=stt["hi"][:], in0=stt["sd"][:], scalar1=1.05, scalar2=None, op0=ALU.mult))
                em.vd()
                em.run("vector", lambda v: v.tensor_tensor(
                    out=stt["lo"][:], in0=stt["lo"][:], in1=stt["mu"][:], op=ALU.add))
                em.run("vector", lambda v: v.tensor_tensor(
                    out=stt["hi"][:], in0=stt["hi"][:], in1=stt["mu"][:], op=ALU.add))
                em.vd()

            def emit_probes():
                gsp = gG // 2
                for it in range(NPROBE):
                    em.run("vector", lambda v: v.tensor_tensor(
                        out=stt["mid"][:], in0=stt["lo"][:], in1=stt["hi"][:], op=ALU.add))
                    em.vd()
                    t_mid = em.run("vector", lambda v: v.tensor_scalar(
                        out=stt["mid"][:], in0=stt["mid"][:], scalar1=0.5, scalar2=None,
                        op0=ALU.mult), sem="vE")
                    t_scnt = None
                    if gsp > 0:
                        em.wait("scalar", "vE", t_mid)
                        for g in range(gG - gsp, gG):
                            em.run("scalar", lambda s, g=g: s.activation(
                                out=es2[1][:], in_=s_g[g][:], func=ACT.Sign,
                                bias=stt["mid"][:, g:g + 1], scale=-1.0,
                                accum_out=stt["sum"][:, g:g + 1]))
                        t_scnt = em.run("scalar", lambda s, gsp=gsp: s.activation(
                            out=stt["cnt"][:, gG - gsp:gG], in_=stt["sum"][:, gG - gsp:gG],
                            func=ACT.Copy, bias=float(SEQ) / 2.0, scale=-0.5), sem="sE")
                    for g in range(gG - gsp):
                        em.run("vector", lambda v, g=g: v.tensor_scalar(
                            out=es2[0][:], in0=s_g[g][:], scalar1=stt["mid"][:, g:g + 1],
                            scalar2=None, op0=ALU.is_ge, op1=ALU.add,
                            accum_out=stt["cnt"][:, g:g + 1]))
                    em.vd()
                    if t_scnt:
                        em.wait("vector", "sE", t_scnt)
                    em.run("vector", lambda v: v.tensor_scalar(
                        out=stt["ge"][:], in0=stt["cnt"][:], scalar1=float(KTOP),
                        scalar2=None, op0=ALU.is_ge))
                    em.run("vector", lambda v: v.tensor_scalar(
                        out=stt["lt"][:], in0=stt["cnt"][:], scalar1=float(KTOP),
                        scalar2=None, op0=ALU.is_lt))
                    em.vd()
                    em.run("vector", lambda v: v.copy_predicated(
                        out=stt["lo"][:], mask=stt["ge"][:], data=stt["mid"][:]))
                    em.run("vector", lambda v: v.copy_predicated(
                        out=stt["hi"][:], mask=stt["lt"][:], data=stt["mid"][:]))
                    em.vd()

            def emit_epilogue(h, gq, boundary, first):
                t_nt = em.run("vector", lambda v: v.tensor_scalar(
                    out=ntg[:], in0=stt["lo"][:], scalar1=-ISC, scalar2=None,
                    op0=ALU.mult), sem="vE")
                em.wait("scalar", "vE", t_nt)
                if first:
                    tap("lo", stt["lo"][:], [128, gG], F32, waits=[("vE", t_nt)])
                    tap("mu", stt["mu"][:], [128, gG], F32, waits=[("vE", t_nt)])
                    tap("sd", stt["sd"][:], [128, gG], F32, waits=[("vE", t_nt)])
                    tap("cnt", stt["cnt"][:], [128, gG], F32, waits=[("vE", t_nt)])
                exp_done = [None] * gG
                pop_done = [None] * gG
                for g in range(min(2, gG)):
                    if tr_prev[g % 2] is not None:
                        em.wait("scalar", "tE", tr_prev[g % 2])
                    exp_done[g] = em.run("scalar", lambda s, g=g: s.activation(
                        out=es2[g % 2][:], in_=s_g[g][:], func=ACT.Exp,
                        bias=ntg[:, g:g + 1], scale=ISC), sem="sE")
                zpc = [None, None]
                for g in range(gG):
                    qt = gq * gG + g
                    em.wait("vector", "sE", exp_done[g])
                    pop_done[g] = em.run("vector", lambda v, g=g: v.scalar_tensor_tensor(
                        out=es2[g % 2][:], in0=s_g[g][:], scalar=stt["lo"][:, g:g + 1],
                        in1=es2[g % 2][:], op0=ALU.is_ge, op1=ALU.mult,
                        accum_out=zrow[:, g:g + 1]), sem="vE")
                    em.run("vector", lambda v, g=g: v.reciprocal(out=rz[:], in_=zrow[:, g:g + 1]))
                    em.vd()
                    t_norm = em.run("vector", lambda v, g=g: v.tensor_scalar(
                        out=es2[g % 2][:], in0=es2[g % 2][:], scalar1=rz[:], scalar2=None,
                        op0=ALU.mult), sem="vE")
                    pc = st_p1.get("pend_ctx")
                    if pc is not None:
                        ph, pqt, ppar, ppv = pc
                        em.wait("vector", "tE", ppv)
                        if ctx_dma[ppar] is not None:
                            em.wait("vector", "gS", ctx_dma[ppar])
                        t_oe = em.run("vector", lambda v, ph=ph, ppar=ppar: v.tensor_scalar(
                            out=ctx_t[ppar][:], in0=p_o[ppar][:], scalar1=bqkv["v"][:, ph:ph + 1],
                            scalar2=None, op0=ALU.add), sem="vE")
                        st_p1["ctx_vE%d" % ppar] = t_oe
                        em.wait("gpsimd", "vE", t_oe)
                        ctx_dma[ppar] = em.dma(
                            "gpsimd",
                            ctxT_hbm.ap()[ph * 128:(ph + 1) * 128, pqt * 128:(pqt + 1) * 128],
                            ctx_t[ppar][:], "gS")
                    em.wait("tensor", "vE", t_norm)
                    for grp in range(ngv):
                        if zpc[grp % 2] is not None:
                            em.wait("tensor", "sE", zpc[grp % 2])
                        for k in range(npk):
                            kc = grp * npk + k
                            em.run("tensor", lambda t, kc=kc, k=k, g=g, grp=grp: t.transpose(
                                out=p_t2[grp % 2][:, 128 * k:128 * (k + 1)],
                                in_=es2[g % 2][:, 128 * kc:128 * (kc + 1)], identity=ident[:]))
                        t_t = em.trun_drain(lambda t: t.nop() if hasattr(t, "nop") else t.engine_nop())
                        em.wait("scalar", "tE", t_t)
                        zpc[grp % 2] = em.run("scalar", lambda s, grp=grp: s.activation(
                            out=zpT[:, 128 * npk * grp:128 * npk * (grp + 1)],
                            in_=p_t2[grp % 2][:, 0:128 * npk], func=ACT.Copy), sem="sE")
                    tr_prev[g % 2] = em.cur("tE")
                    if g + 2 < gG:
                        g2 = g + 2
                        em.wait("scalar", "tE", tr_prev[g % 2])
                        exp_done[g2] = em.run("scalar", lambda s, g2=g2: s.activation(
                            out=es2[g2 % 2][:], in_=s_g[g2][:], func=ACT.Exp,
                            bias=ntg[:, g2:g2 + 1], scale=ISC), sem="sE")
                    em.wait("tensor", "sE", zpc[(ngv - 1) % 2])
                    cv = st_p1.get("ctx_vE%d" % (g % 2))
                    if cv is not None:
                        em.wait("tensor", "vE", cv)
                    t_mm = None
                    for kc in range(nkv):
                        t_mm = em.run("tensor", lambda t, kc=kc, g=g: t.matmul(
                            out=p_o[g % 2][:], lhsT=v_h[:, 128 * kc:128 * (kc + 1)],
                            rhs=zpT[:, 128 * kc:128 * (kc + 1)],
                            start=(kc == 0), stop=(kc == nkv - 1)), sem="tE")
                    st_p1["pend_ctx"] = (h, qt, g % 2, t_mm)
                    if DBG:
                        em.run("vector", lambda v, h=h, gq=gq, g=g: v.tensor_copy(
                            out=z_all[:, (h * ngq + gq) * gG + g:(h * ngq + gq) * gG + g + 1],
                            in_=zrow[:, g:g + 1]))
                    if first and g == 0:
                        tap("zrow", zrow[:], [128, gG], F32, waits=[("vE", em.cur("vE"))])
                if DBG:
                    em.run("vector", lambda v, h=h, gq=gq: v.tensor_copy(
                        out=lo_all[:, (h * ngq + gq) * gG:(h * ngq + gq + 1) * gG],
                        in_=stt["lo"][:]))
                if boundary and h + 1 < NH:
                    load_kv(h + 1)
                    stage_v(h + 1)
                return pop_done, exp_done

            load_kv(0)
            stage_v(0)
            t_sq = emit_scores(0, 0, None)
            for h in range(NH):
                for gq in range(ngq):
                    first = (h == 0 and gq == 0)
                    emit_stats(t_sq, first)
                    emit_probes()
                    boundary = (gq == ngq - 1)
                    pop_done = emit_epilogue(h, gq, boundary, first)
                    if not (h == NH - 1 and boundary):
                        nh, ngq2 = (h, gq + 1) if not boundary else (h + 1, 0)
                        t_sq = emit_scores(nh, ngq2, pop_done)

            pc = st_p1.get("pend_ctx")
            if pc is not None:
                ph, pqt, ppar, ppv = pc
                em.wait("vector", "tE", ppv)
                if ctx_dma[ppar] is not None:
                    em.wait("vector", "gS", ctx_dma[ppar])
                t_oe = em.run("vector", lambda v, ph=ph, ppar=ppar: v.tensor_scalar(
                    out=ctx_t[ppar][:], in0=p_o[ppar][:], scalar1=bqkv["v"][:, ph:ph + 1],
                    scalar2=None, op0=ALU.add), sem="vE")
                em.wait("gpsimd", "vE", t_oe)
                ctx_dma[ppar] = em.dma(
                    "gpsimd", ctxT_hbm.ap()[ph * 128:(ph + 1) * 128, pqt * 128:(pqt + 1) * 128],
                    ctx_t[ppar][:], "gS")

            if DBG:
                t_fin = em.run("vector", lambda v: v.engine_nop(), sem="vE")
                tap("loall", lo_all[:], [128, NH * NTT], F32, waits=[("vE", t_fin)])
                tap("zall", z_all[:], [128, NH * NTT], F32, waits=[("vE", t_fin)])

        st01.close()
        r_sb = ec(nc.sbuf_tensor("r_sb", [128, NTT * D], F32))
        ctx_done = em.cur("gS")

        with ExitStack() as st2:
            ec2 = st2.enter_context
            lhs_b = [ec2(nc.sbuf_tensor(f"lhs_b{i}", [128, 512], BF16)) for i in range(2)]
            wb1 = [ec2(nc.sbuf_tensor(f"wb1_{i}", [128, 512], BF16)) for i in range(2)]
            wb2 = [ec2(nc.sbuf_tensor(f"wb2_{i}", [128, 512], BF16)) for i in range(2)]
            scr = ec2(nc.sbuf_tensor("scr", [128, D], F32))
            scr8 = ec2(nc.sbuf_tensor("scr8", [128, D], FP8))
            scrb = ec2(nc.sbuf_tensor("scrb", [128, D], BF16))
            mstat = ec2(nc.sbuf_tensor("mstat", [128, NTT], F32))
            rstat = ec2(nc.sbuf_tensor("rstat", [128, NTT], F32))
            amx = ec2(nc.sbuf_tensor("amx", [128, 1], F32))
            amx2 = ec2(nc.sbuf_tensor("amx2", [128, 1], F32))

            def matmul_sweep(lhsT_hbm, bias_sb, wkey, evac, lhs_guard, bias_guard):
                with ExitStack() as stS:
                    pmm = [stS.enter_context(nc.psum_tensor(f"pS_{wkey}{i}", [128, 512], F32))
                           for i in range(8)]
                    em.wait("sync", "gS", lhs_guard)
                    em.barrier("sync", sems=("tE", "vE", "sE"))
                    em.wait("tensor", "vE", bias_guard)
                    lfree = [None, None]
                    wfree = [None, None]
                    nquad = max(1, NTT // 4)
                    ntq = min(4, NTT)
                    npair = NCORES // 2
                    for quad in range(nquad):
                        for pair in range(npair):
                            r1, r2 = 2 * pair, 2 * pair + 1
                            t_mm = None
                            for di in range(nd):
                                par = di % 2
                                em.wait("sync", "tE", lfree[par])
                                em.dma("sync", lhs_b[par][:, 0:128 * ntq],
                                       lhsT_hbm[di * 128:(di + 1) * 128,
                                                quad * 128 * ntq:(quad + 1) * 128 * ntq], "dO")
                                em.wait("sync", "tE", wfree[par])
                                em.dma("sync", wb1[par][:, 0:WSL],
                                       wT_out[wkey].ap()[r1 * D + di * 128: r1 * D + (di + 1) * 128, :],
                                       "dO")
                                t_l = em.dma("sync", wb2[par][:, 0:WSL],
                                             wT_out[wkey].ap()[r2 * D + di * 128: r2 * D + (di + 1) * 128, :],
                                             "dO")
                                em.wait("tensor", "dO", t_l)
                                for tt4 in range(ntq):
                                    em.run("tensor", lambda t, par=par, tt4=tt4, di=di: t.matmul(
                                        out=pmm[2 * tt4][:, 0:WSL],
                                        lhsT=lhs_b[par][:, 128 * tt4:128 * (tt4 + 1)],
                                        rhs=wb1[par][:, 0:WSL], start=(di == 0), stop=False))
                                    t_mm = em.run("tensor", lambda t, par=par, tt4=tt4, di=di: t.matmul(
                                        out=pmm[2 * tt4 + 1][:, 0:WSL],
                                        lhsT=lhs_b[par][:, 128 * tt4:128 * (tt4 + 1)],
                                        rhs=wb2[par][:, 0:WSL], start=(di == 0), stop=False), sem="tE")
                                lfree[par] = t_mm
                                wfree[par] = t_mm
                            for tt4 in range(ntq):
                                em.run("tensor", lambda t, r1=r1, tt4=tt4: t.matmul(
                                    out=pmm[2 * tt4][:, 0:WSL], lhsT=ones_row[:],
                                    rhs=bias_sb[0:1, WSL * r1:WSL * (r1 + 1)],
                                    start=False, stop=True))
                                t_mm = em.run("tensor", lambda t, r2=r2, tt4=tt4: t.matmul(
                                    out=pmm[2 * tt4 + 1][:, 0:WSL], lhsT=ones_row[:],
                                    rhs=bias_sb[0:1, WSL * r2:WSL * (r2 + 1)],
                                    start=False, stop=True), sem="tE")
                            for tt4 in range(ntq):
                                evac(quad * ntq + tt4, pair, pmm[2 * tt4], pmm[2 * tt4 + 1], t_mm)
                    em.wait("tensor", "vE", em.cur("vE"))

            def o_evac(tt, pair, pa, pb, t_mm):
                em.wait("vector", "tE", t_mm)
                t_v = None
                for which, pq in ((0, pa), (1, pb)):
                    dcol = (2 * pair + which) * WSL
                    t_v = em.run("vector", lambda v, pq=pq, tt=tt, dcol=dcol: v.tensor_copy(
                        out=r_sb[:, tt * D + dcol: tt * D + dcol + WSL], in_=pq[:, 0:WSL]), sem="vE")
                em.wait("tensor", "vE", t_v)
            tap("ctxT", ctxT_hbm.ap(), [D, TOK], BF16, eng="gpsimd",
                waits=[("gS", ctx_done)])
            matmul_sweep(ctxT_hbm.ap(), bo_sb, "o", o_evac, ctx_done, bo_cast_v)
            tap("osum", r_sb[:, 0:NTT * D], [128, NTT * D], F32, waits=[("vE", em.cur("vE"))])

            scr_free = None
            for tt in range(NTT):
                em.wait("sync", "vE", scr_free)
                t_l = em.dma("sync", scr[:], x_hbm[tt * 128:(tt + 1) * 128, :], "dO")
                em.wait("vector", "dO", t_l)
                t_add = em.run("vector", lambda v, tt=tt: v.tensor_tensor(
                    out=r_sb[:, tt * D:(tt + 1) * D], in0=r_sb[:, tt * D:(tt + 1) * D],
                    in1=scr[:], op=ALU.add), sem="vE")
                em.vd()
                em.run("vector", lambda v, tt=tt: v.tensor_reduce(
                    out=mstat[:, tt:tt + 1], in_=r_sb[:, tt * D:(tt + 1) * D], axis=AXX, op=ALU.add))
                em.wait("scalar", "vE", t_add)
                t_s = em.run("scalar", lambda s, tt=tt: s.activation(
                    out=scrb[:], in_=r_sb[:, tt * D:(tt + 1) * D], func=ACT.Square,
                    accum_out=rstat[:, tt:tt + 1]), sem="sE")
                em.wait("vector", "sE", t_s)
                em.vd()
                em.run("vector", lambda v, tt=tt: v.tensor_scalar(
                    out=mstat[:, tt:tt + 1], in0=mstat[:, tt:tt + 1], scalar1=1.0 / D,
                    scalar2=None, op0=ALU.mult))
                em.vd()
                em.run("vector", lambda v, tt=tt: v.tensor_tensor(
                    out=amx2[:], in0=mstat[:, tt:tt + 1], in1=mstat[:, tt:tt + 1], op=ALU.mult))
                em.run("vector", lambda v, tt=tt: v.tensor_scalar(
                    out=rstat[:, tt:tt + 1], in0=rstat[:, tt:tt + 1], scalar1=1.0 / D,
                    scalar2=None, op0=ALU.mult))
                em.vd()
                em.run("vector", lambda v, tt=tt: v.tensor_tensor(
                    out=rstat[:, tt:tt + 1], in0=rstat[:, tt:tt + 1], in1=amx2[:], op=ALU.subtract))
                em.vd()
                t_v = em.run("vector", lambda v, tt=tt: v.tensor_scalar(
                    out=rstat[:, tt:tt + 1], in0=rstat[:, tt:tt + 1], scalar1=1e-5,
                    scalar2=None, op0=ALU.add), sem="vE")
                em.wait("scalar", "vE", t_v)
                t_s = em.run("scalar", lambda s, tt=tt: s.activation(
                    out=rstat[:, tt:tt + 1], in_=rstat[:, tt:tt + 1], func=ACT.Sqrt), sem="sE")
                em.wait("vector", "sE", t_s)
                em.run("vector", lambda v, tt=tt: v.reciprocal(
                    out=rstat[:, tt:tt + 1], in_=rstat[:, tt:tt + 1]))
                em.vd()
                em.run("vector", lambda v, tt=tt: v.tensor_scalar(
                    out=scr[:], in0=r_sb[:, tt * D:(tt + 1) * D], scalar1=mstat[:, tt:tt + 1],
                    scalar2=None, op0=ALU.subtract))
                scr_free = em.run("vector", lambda v: v.tensor_reduce(
                    out=amx2[:], in_=scr[:], axis=AXX, op=ALU.max, apply_absolute_value=True), sem="vE")
                em.vd()
                em.run("vector", lambda v, tt=tt: v.tensor_tensor(
                    out=amx2[:], in0=amx2[:], in1=rstat[:, tt:tt + 1], op=ALU.mult))
                em.vd()
                if tt == 0:
                    em.run("vector", lambda v: v.tensor_copy(out=amx[:], in_=amx2[:]))
                else:
                    em.run("vector", lambda v: v.tensor_tensor(
                        out=amx[:], in0=amx[:], in1=amx2[:], op=ALU.max))
                em.vd()

            tap("r1", r_sb[:, 0:NTT * D], [128, NTT * D], F32, waits=[("vE", em.cur("vE"))])
            tap("mst", mstat[:], [128, NTT], F32, waits=[("vE", em.cur("vE"))])
            tap("rst", rstat[:], [128, NTT], F32, waits=[("vE", em.cur("vE"))])
            tap("amx", amx[:], [128, 1], F32, waits=[("vE", em.cur("vE"))])
            t_v = em.run("vector", lambda v: v.engine_nop(), sem="vE")
            em.wait("gpsimd", "vE", t_v)
            t_g = em.dma("gpsimd", red_hbm.ap(), amx[:], "gS")
            em.wait("gpsimd", "gS", t_g)
            t_g = em.dma("gpsimd", red_sb[:], red_hbm.ap(), "gS")
            em.wait("vector", "gS", t_g)
            em.run("vector", lambda v: v.tensor_reduce(
                out=one_sc[0:1, 1:2], in_=red_sb[0:1, :], axis=AXX, op=ALU.max))
            em.vd()
            em.run("vector", lambda v: v.tensor_copy(
                out=one_sc[0:1, 0:8], in_=one_sc[0:1, 1:2].to_broadcast([1, 8])))
            t_v = em.run("vector", lambda v: v.engine_nop(), sem="vE")
            em.wait("gpsimd", "vE", t_v)
            t_g = em.dma("gpsimd", ax_in[0:1, 0:8], one_sc[0:1, 0:8], "gS")
            em.wait("gpsimd", "gS", t_g)
            t_cc = em.run("gpsimd", lambda g: g.collective_compute(
                "AllReduce", ALU.max, replica_groups=[list(range(8))],
                ins=[ax_in.ap().opt()], outs=[ax_out.ap().opt()]), sem="cc")
            em.wait("gpsimd", "cc", t_cc)
            t_g = em.dma("gpsimd", sxi2[:], ax_out[0:1, 0:1].to_broadcast([128, 1]), "gS")
            em.wait("vector", "gS", t_g)
            em.run("vector", lambda v: v.tensor_scalar(
                out=sxi2[:], in0=sxi2[:], scalar1=1e-12, scalar2=1.0 / 448.0,
                op0=ALU.max, op1=ALU.mult))
            em.vd()
            em.run("vector", lambda v: v.tensor_tensor(
                out=dq_sc[:], in0=sxi2[:], in1=sw_sc[:], op=ALU.mult))
            em.vd()
            em.run("vector", lambda v: v.tensor_scalar(
                out=dq_sc[:], in0=dq_sc[:], scalar1=4.0, scalar2=None, op0=ALU.mult))
            em.run("vector", lambda v: v.tensor_scalar(
                out=sxi2[:], in0=sxi2[:], scalar1=2.0, scalar2=None, op0=ALU.mult))
            em.vd()
            em.run("vector", lambda v: v.reciprocal(out=sxi2[:], in_=sxi2[:]))
            em.run("vector", lambda v: v.reciprocal(out=amx2[:], in_=dq_sc[:]))
            em.vd()
            bfs_v = em.run("vector", lambda v: v.tensor_scalar(
                out=bfs_sb[:], in0=bfr_sb[:], scalar1=amx2[0:1, 0:1], scalar2=None,
                op0=ALU.mult), sem="vE")

            with ExitStack() as stq:
                ptr2 = stq.enter_context(nc.psum_tensor("ptr2", [128, 1024], BF16))
                for tt in range(NTT):
                    em.wait("vector", "gS", em.cur("gS"))
                    em.run("vector", lambda v, tt=tt: v.tensor_scalar(
                        out=scr[:], in0=r_sb[:, tt * D:(tt + 1) * D], scalar1=mstat[:, tt:tt + 1],
                        scalar2=rstat[:, tt:tt + 1], op0=ALU.subtract, op1=ALU.mult))
                    em.vd()
                    em.run("vector", lambda v, tt=tt: v.tensor_copy(
                        out=r_sb[:, tt * D:(tt + 1) * D], in_=scr[:]))
                    em.run("vector", lambda v: v.tensor_scalar(
                        out=scr8[:], in0=scr[:], scalar1=sxi2[:], scalar2=None, op0=ALU.mult))
                    em.vd()
                    t_c = em.run("vector", lambda v: v.tensor_copy(out=scrb[:], in_=scr8[:]), sem="vE")
                    em.wait("tensor", "vE", t_c)
                    for grp in range(ngrp):
                        for k in range(8):
                            i = grp * 8 + k
                            em.run("tensor", lambda t, i=i, k=k: t.transpose(
                                out=ptr2[:, 128 * k:128 * (k + 1)], in_=scrb[:, 128 * i:128 * (i + 1)],
                                identity=ident[:]))
                        t_t = em.trun_drain(lambda t: t.nop() if hasattr(t, "nop") else t.engine_nop())
                        em.wait("vector", "tE", t_t)
                        t_e = em.run("vector", lambda v, grp=grp: v.tensor_copy(
                            out=scrb[:, 1024 * grp:1024 * (grp + 1)], in_=ptr2[:]), sem="vE")
                        em.wait("tensor", "vE", t_e)
                    em.wait("gpsimd", "vE", em.cur("vE"))
                    for half in range(2):
                        hh = nd // 2
                        em.dma("gpsimd",
                               qxT_hbm.ap()[:, tt * 128:(tt + 1) * 128]
                               .rearrange("(c p) w -> p c w", p=128)
                               [:, hh * half:hh * (half + 1)],
                               scrb[:, :].rearrange("p (c w) -> p c w", w=128)
                               [:, hh * half:hh * (half + 1)], "gS")

            tap("sxdq", dq_sc[:], [128, 1], F32, waits=[("vE", em.cur("vE"))])
            tap("y0", r_sb[:, 0:NTT * D], [128, NTT * D], F32, waits=[("vE", em.cur("vE"))])
            def f_evac(tt, pair, pa, pb, t_mm):
                em.wait("vector", "tE", t_mm)
                t_v = None
                for which, pq in ((0, pa), (1, pb)):
                    dcol = (2 * pair + which) * WSL
                    t_v = em.run("vector", lambda v, pq=pq, tt=tt, dcol=dcol: v.scalar_tensor_tensor(
                        out=r_sb[:, tt * D + dcol: tt * D + dcol + WSL], in0=pq[:, 0:WSL],
                        scalar=dq_sc[:], in1=r_sb[:, tt * D + dcol: tt * D + dcol + WSL],
                        op0=ALU.mult, op1=ALU.add), sem="vE")
                em.wait("tensor", "vE", t_v)
            matmul_sweep(qxT_hbm.ap(), bfs_sb, "f", f_evac, em.cur("gS"), bfs_v)

            tap("r2", r_sb[:, 0:NTT * D], [128, NTT * D], F32, waits=[("vE", em.cur("vE"))])
            for tt in range(NTT):
                em.run("vector", lambda v, tt=tt: v.tensor_reduce(
                    out=mstat[:, tt:tt + 1], in_=r_sb[:, tt * D:(tt + 1) * D], axis=AXX, op=ALU.add))
                t_v = em.run("vector", lambda v: v.engine_nop(), sem="vE")
                em.wait("scalar", "vE", t_v)
                t_s = em.run("scalar", lambda s, tt=tt: s.activation(
                    out=scrb[:], in_=r_sb[:, tt * D:(tt + 1) * D], func=ACT.Square,
                    accum_out=rstat[:, tt:tt + 1]), sem="sE")
                em.wait("vector", "sE", t_s)
                em.vd()
                em.run("vector", lambda v, tt=tt: v.tensor_scalar(
                    out=mstat[:, tt:tt + 1], in0=mstat[:, tt:tt + 1], scalar1=1.0 / D,
                    scalar2=None, op0=ALU.mult))
                em.vd()
                em.run("vector", lambda v, tt=tt: v.tensor_tensor(
                    out=amx2[:], in0=mstat[:, tt:tt + 1], in1=mstat[:, tt:tt + 1], op=ALU.mult))
                em.run("vector", lambda v, tt=tt: v.tensor_scalar(
                    out=rstat[:, tt:tt + 1], in0=rstat[:, tt:tt + 1], scalar1=1.0 / D,
                    scalar2=None, op0=ALU.mult))
                em.vd()
                em.run("vector", lambda v, tt=tt: v.tensor_tensor(
                    out=rstat[:, tt:tt + 1], in0=rstat[:, tt:tt + 1], in1=amx2[:], op=ALU.subtract))
                em.vd()
                t_v = em.run("vector", lambda v, tt=tt: v.tensor_scalar(
                    out=rstat[:, tt:tt + 1], in0=rstat[:, tt:tt + 1], scalar1=1e-5,
                    scalar2=None, op0=ALU.add), sem="vE")
                em.wait("scalar", "vE", t_v)
                t_s = em.run("scalar", lambda s, tt=tt: s.activation(
                    out=rstat[:, tt:tt + 1], in_=rstat[:, tt:tt + 1], func=ACT.Sqrt), sem="sE")
                em.wait("vector", "sE", t_s)
                em.run("vector", lambda v, tt=tt: v.reciprocal(
                    out=rstat[:, tt:tt + 1], in_=rstat[:, tt:tt + 1]))
                em.vd()
                em.wait("vector", "dOut", em.cur("dOut"))
                t_v = em.run("vector", lambda v, tt=tt: v.tensor_scalar(
                    out=scr[:], in0=r_sb[:, tt * D:(tt + 1) * D], scalar1=mstat[:, tt:tt + 1],
                    scalar2=rstat[:, tt:tt + 1], op0=ALU.subtract, op1=ALU.mult), sem="vE")
                em.wait("sync", "vE", t_v)
                em.dma("sync", out_hbm[tt * 128:(tt + 1) * 128, :], scr[:], "dOut")

            em.wait("sync", "dOut", em.cur("dOut"))

        with nc.Block() as block:
            @block.sync
            def _(e):
                for fn in em.ops["sync"]:
                    fn(e)

            @block.gpsimd
            def _(e):
                for fn in em.ops["gpsimd"]:
                    fn(e)

            @block.vector
            def _(e):
                for fn in em.ops["vector"]:
                    fn(e)

            @block.scalar
            def _(e):
                for fn in em.ops["scalar"]:
                    fn(e)

            @block.tensor
            def _(e):
                for fn in em.ops["tensor"]:
                    fn(e)

    return nc


_NC_CACHE = None


def kernel(L0_attn_hidden_states, L0_attn_wq, L0_attn_bq, L0_attn_wk, L0_attn_bk,
           L0_attn_wv, L0_attn_bv, L0_attn_wo, L0_attn_bo, L0_ffn_W, L0_ffn_b):
    global _NC_CACHE
    x = np.ascontiguousarray(np.asarray(L0_attn_hidden_states, dtype=np.float32)).reshape(B * SEQ, D)
    ws = {"q": np.asarray(L0_attn_wq, np.float32), "k": np.asarray(L0_attn_wk, np.float32),
          "v": np.asarray(L0_attn_wv, np.float32), "o": np.asarray(L0_attn_wo, np.float32),
          "f": np.asarray(L0_ffn_W, np.float32)}
    bq = np.ascontiguousarray(np.asarray(L0_attn_bq, np.float32).reshape(D, 1))
    bk = np.ascontiguousarray(np.asarray(L0_attn_bk, np.float32).reshape(D, 1))
    bv = np.ascontiguousarray(np.asarray(L0_attn_bv, np.float32).reshape(D, 1))
    bo = np.ascontiguousarray(np.asarray(L0_attn_bo, np.float32).reshape(1, D))
    bf = np.ascontiguousarray(np.asarray(L0_ffn_b, np.float32).reshape(1, D))

    if _NC_CACHE is None:
        _NC_CACHE = build_nc()
    nc = _NC_CACHE

    in_maps = []
    for c in range(NCORES):
        m = {"x_slab": np.ascontiguousarray(x[c * TOK:(c + 1) * TOK]),
             "bq_col": bq, "bk_col": bk, "bv_col": bv, "bo_row": bo, "bf_row": bf}
        for k, w in ws.items():
            m[f"w{k}_sl"] = np.ascontiguousarray(w[c * WSL:(c + 1) * WSL])
        in_maps.append(m)

    res = run_bass_kernel_spmd(nc, in_maps, core_ids=list(range(NCORES)))
    out = np.concatenate([np.asarray(r["out_slab"]) for r in res.results], axis=0)
    return np.ascontiguousarray(out.reshape(B, SEQ, D).astype(np.float32))


# version 20
# speedup vs baseline: 1.0817x; 1.0524x over previous
import math
from contextlib import ExitStack

import numpy as np

import concourse.bass as bass
import concourse.mybir as mybir
from concourse.bass_utils import run_bass_kernel_spmd

F32 = mybir.dt.float32
BF16 = mybir.dt.bfloat16
FP8 = mybir.dt.float8e4
ALU = mybir.AluOpType
ACT = mybir.ActivationFunctionType
AXX = mybir.AxisListType.X

NCORES = 8
B = 2
SEQ = 4096
D = 4096
NH = 32
DH = 128
TOK = 1024
NTT = TOK // 128
KTOP = 819
ISC = 1.0 / math.sqrt(DH)
NPROBE = 4
G = 4
WSL = D // NCORES
DBG = False


class Emit:
    ENGINES = ("sync", "gpsimd", "vector", "scalar", "tensor")

    def __init__(self, nc):
        self.nc = nc
        self.ops = {e: [] for e in self.ENGINES}
        self.ctr = {}
        self.sems = {}

    def on(self, eng, fn):
        self.ops[eng].append(fn)

    def cur(self, sem):
        return self.ctr.get(sem, 0)

    def dma(self, eng, out, in_, sem):
        s = self.sems[sem]
        self.ctr[sem] = self.cur(sem) + 16
        tgt = self.ctr[sem]
        self.on(eng, lambda e, out=out, in_=in_, s=s: e.dma_start(out=out, in_=in_).then_inc(s, 16))
        return tgt

    def wait(self, eng, sem, tgt):
        if not tgt:
            return
        s = self.sems[sem]
        self.on(eng, lambda e, s=s, tgt=tgt: e.wait_ge(s, tgt))

    def vd(self):
        self.on("vector", lambda e: e.drain())

    def trun_drain(self, fn, sem="tE"):
        s = self.sems[sem]
        self.ctr[sem] = self.cur(sem) + 1
        tgt = self.ctr[sem]
        def wrapped(e, fn=fn, s=s):
            fn(e)
            e.drain().then_inc(s, 1)
        self.on("tensor", wrapped)
        return tgt

    def barrier(self, eng, sems=("tE", "vE", "sE", "gS")):
        for s in sems:
            self.wait(eng, s, self.cur(s))

    def run(self, eng, fn, sem=None):
        if sem is None:
            self.on(eng, fn)
            return None
        s = self.sems[sem]
        self.ctr[sem] = self.cur(sem) + 1
        tgt = self.ctr[sem]
        if eng in ("vector", "scalar"):
            def wrapped(e, fn=fn, s=s):
                fn(e)
                e.drain().then_inc(s, 1)
            self.on(eng, wrapped)
        else:
            self.on(eng, lambda e, fn=fn, s=s: fn(e).then_inc(s, 1))
        return tgt


def build_nc():
    nd = D // 128
    ngrp = nd // 8
    nkc = SEQ // 512
    nkv = SEQ // 128
    ndo = D // 512
    nc = bass.Bass("TRN2", target_bir_lowering=False, detect_race_conditions=False)

    x_hbm = nc.dram_tensor("x_slab", [TOK, D], F32, kind="ExternalInput")
    w_sl = {m: nc.dram_tensor(f"w{m}_sl", [WSL, D], F32, kind="ExternalInput")
            for m in ("q", "k", "v", "o", "f")}
    b_col = {m: nc.dram_tensor(f"b{m}_col", [D, 1], F32, kind="ExternalInput")
             for m in ("q", "k", "v")}
    bo_row = nc.dram_tensor("bo_row", [1, D], F32, kind="ExternalInput")
    bf_row = nc.dram_tensor("bf_row", [1, D], F32, kind="ExternalInput")
    out_hbm = nc.dram_tensor("out_slab", [TOK, D], F32, kind="ExternalOutput")

    wT_in = {m: nc.dram_tensor(f"w{m}T_in", [D, WSL], BF16) for m in ("q", "k", "v", "o", "f")}
    wT_out = {m: nc.dram_tensor(f"w{m}T_out", [NCORES * D, WSL], BF16) for m in ("q", "k", "v", "o", "f")}
    kT_in = nc.dram_tensor("kT_in", [D, TOK], BF16)
    kT_out = nc.dram_tensor("kT_out", [4 * D, TOK], BF16)
    v_in = nc.dram_tensor("v_in", [D, TOK], BF16)
    v_out = nc.dram_tensor("v_out", [4 * D, TOK], BF16)
    ctxT_hbm = nc.dram_tensor("ctxT_hbm", [D, TOK], BF16)
    qxT_hbm = nc.dram_tensor("qxT_hbm", [D, TOK], BF16)
    red_hbm = nc.dram_tensor("red_hbm", [1, 128], F32)
    am_in = nc.dram_tensor("am_in", [1, 8], F32)
    am_out = nc.dram_tensor("am_out", [1, 8], F32)
    ax_in = nc.dram_tensor("ax_in", [1, 8], F32)
    ax_out = nc.dram_tensor("ax_out", [1, 8], F32)

    em = Emit(nc)

    dbg = {}

    def tap(name, ap, shape, dtype, eng="sync", waits=()):
        if not DBG:
            return
        t = nc.dram_tensor(f"dbg_{name}", shape, dtype, kind="ExternalOutput")
        dbg[name] = t
        for sem, tgt in waits:
            em.wait(eng, sem, tgt)
        em.dma(eng, t.ap(), ap, "dOut")

    with ExitStack() as st:
        ec = st.enter_context
        for name in ("dW", "dA", "dB", "dK", "dV", "dO", "dOut", "gS", "cc",
                     "tE", "vE", "sE", "gE"):
            em.sems[name] = ec(nc.semaphore(name))

        ident = ec(nc.sbuf_tensor("ident", [128, 128], BF16))
        iota_r = ec(nc.sbuf_tensor("iota_r", [128, 128], F32))
        iota_c = ec(nc.sbuf_tensor("iota_c", [128, 128], F32))
        ones_row = ec(nc.sbuf_tensor("ones_row", [1, 128], BF16))
        bqkv = {m: ec(nc.sbuf_tensor(f"b{m}_sb", [128, NH], F32)) for m in ("q", "k", "v")}
        bo_sb = ec(nc.sbuf_tensor("bo_sb", [1, D], BF16))
        bfr_sb = ec(nc.sbuf_tensor("bfr_sb", [1, D], F32))
        bfs_sb = ec(nc.sbuf_tensor("bfs_sb", [1, D], BF16))
        sw_sc = ec(nc.sbuf_tensor("sw_sc", [128, 1], F32))
        swi2 = ec(nc.sbuf_tensor("swi2", [128, 1], F32))
        sxi2 = ec(nc.sbuf_tensor("sxi2", [128, 1], F32))
        dq_sc = ec(nc.sbuf_tensor("dq_sc", [128, 1], F32))
        red_sb = ec(nc.sbuf_tensor("red_sb", [1, 128], F32))
        one_sc = ec(nc.sbuf_tensor("one_sc", [1, 8], F32))

        em.run("gpsimd", lambda g: g.iota(iota_r[:], pattern=[[0, 128]], base=0,
                                          channel_multiplier=1, allow_small_or_imprecise_dtypes=True))
        t_iota = em.run("gpsimd", lambda g: g.iota(iota_c[:], pattern=[[1, 128]], base=0,
                                                   channel_multiplier=0,
                                                   allow_small_or_imprecise_dtypes=True), sem="gE")
        em.wait("vector", "gE", t_iota)
        em.run("vector", lambda v: v.tensor_tensor(out=ident[:], in0=iota_r[:], in1=iota_c[:],
                                                   op=ALU.is_equal))
        em.run("vector", lambda v: v.memset(ones_row[:], 1.0))

        for m in ("q", "k", "v"):
            for h in range(NH):
                em.dma("sync", bqkv[m][:, h:h + 1], b_col[m][128 * h:128 * (h + 1), 0:1], "dB")
        t_bias = em.dma("sync", bfr_sb[:], bf_row.ap(), "dB")
        em.wait("scalar", "dB", t_bias)
        em.wait("vector", "dB", t_bias)

        cc_tgt = {}
        bo_cast_v = None
        with ExitStack() as stW:
            ecW = stW.enter_context
            wrow = [ecW(nc.sbuf_tensor(f"wrow{i}", [128, D], F32)) for i in range(2)]
            wbf = ecW(nc.sbuf_tensor("wbf", [128, D], BF16))
            wq8 = ecW(nc.sbuf_tensor("wq8", [128, D], FP8))
            wTsb = ecW(nc.sbuf_tensor("wTsb", [128, 32 * WSL], BF16))
            pW = [ecW(nc.psum_tensor("pW%d" % i, [128, 1024], BF16)) for i in range(2)]
            amw = ecW(nc.sbuf_tensor("amw", [128, 1], F32))
            amw2 = ecW(nc.sbuf_tensor("amw2", [128, 1], F32))
            bo_f32 = ecW(nc.sbuf_tensor("bo_f32", [1, D], F32))

            t_bo = em.dma("sync", bo_f32[:], bo_row.ap(), "dB")
            em.wait("vector", "dB", t_bo)
            bo_cast_v = em.run("vector", lambda v: v.tensor_copy(out=bo_sb[:], in_=bo_f32[:]), sem="vE")

            wrow_free = [None, None]

            def load_wrow(m, j):
                buf = wrow[j % 2]
                em.wait("sync", "vE", wrow_free[j % 2])
                t_l = em.dma("sync", buf[:], w_sl[m][128 * j:128 * (j + 1), :], "dW")
                em.wait("vector", "dW", t_l)
                return buf

            def preprocess_weight(m, quant):
                if quant:
                    t_c = None
                    for j in range(WSL // 128):
                        buf = load_wrow(m, j)
                        if j == 0:
                            t_c = em.run("vector", lambda v, b=buf: v.tensor_reduce(
                                out=amw[:], in_=b[:], axis=AXX, op=ALU.max,
                                apply_absolute_value=True), sem="vE")
                        else:
                            em.run("vector", lambda v, b=buf: v.tensor_reduce(
                                out=amw2[:], in_=b[:], axis=AXX, op=ALU.max,
                                apply_absolute_value=True))
                            em.vd()
                            t_c = em.run("vector", lambda v: v.tensor_tensor(
                                out=amw[:], in0=amw[:], in1=amw2[:], op=ALU.max), sem="vE")
                            em.vd()
                        wrow_free[j % 2] = t_c
                    em.wait("gpsimd", "vE", t_c)
                    t_g = em.dma("gpsimd", red_hbm.ap(), amw[:], "gS")
                    em.wait("gpsimd", "gS", t_g)
                    t_g = em.dma("gpsimd", red_sb[:], red_hbm.ap(), "gS")
                    em.wait("vector", "gS", t_g)
                    em.run("vector", lambda v: v.tensor_reduce(
                        out=one_sc[0:1, 0:1], in_=red_sb[0:1, :], axis=AXX, op=ALU.max))
                    em.vd()
                    em.run("vector", lambda v: v.tensor_copy(
                        out=one_sc[0:1, 0:8], in_=one_sc[0:1, 0:1].to_broadcast([1, 8])))
                    t_v = em.run("vector", lambda v: v.engine_nop(), sem="vE")
                    em.wait("gpsimd", "vE", t_v)
                    t_g = em.dma("gpsimd", am_in[0:1, 0:8], one_sc[0:1, 0:8], "gS")
                    em.wait("gpsimd", "gS", t_g)
                    t_cc = em.run("gpsimd", lambda g: g.collective_compute(
                        "AllReduce", ALU.max, replica_groups=[list(range(8))],
                        ins=[am_in.ap().opt()], outs=[am_out.ap().opt()]), sem="cc")
                    em.wait("gpsimd", "cc", t_cc)
                    t_g = em.dma("gpsimd", sw_sc[:], am_out[0:1, 0:1].to_broadcast([128, 1]), "gS")
                    em.wait("vector", "gS", t_g)
                    em.run("vector", lambda v: v.tensor_scalar(
                        out=sw_sc[:], in0=sw_sc[:], scalar1=1e-12, scalar2=1.0 / 448.0,
                        op0=ALU.max, op1=ALU.mult))
                    em.vd()
                    em.run("vector", lambda v: v.tensor_scalar(
                        out=swi2[:], in0=sw_sc[:], scalar1=2.0, scalar2=None, op0=ALU.mult))
                    em.vd()
                    em.run("vector", lambda v: v.reciprocal(out=swi2[:], in_=swi2[:]))
                    em.vd()

                ag_guard = em.cur("gS")
                for j in range(WSL // 128):
                    buf = load_wrow(m, j)
                    if quant:
                        em.run("vector", lambda v, b=buf: v.tensor_scalar(
                            out=wq8[:], in0=b[:], scalar1=swi2[:], scalar2=None, op0=ALU.mult))
                        em.vd()
                        t_c = em.run("vector", lambda v: v.tensor_copy(out=wbf[:], in_=wq8[:]), sem="vE")
                    else:
                        t_c = em.run("vector", lambda v, b=buf: v.tensor_copy(out=wbf[:], in_=b[:]), sem="vE")
                    wrow_free[j % 2] = t_c
                    em.wait("tensor", "vE", t_c)
                    if j == 0:
                        em.wait("vector", "gS", ag_guard)
                    for grp in range(ngrp):
                        pt = pW[grp % 2]
                        for k in range(8):
                            i = grp * 8 + k
                            em.run("tensor", lambda t, i=i, k=k, pt=pt: t.transpose(
                                out=pt[:, 128 * k:128 * (k + 1)], in_=wbf[:, 128 * i:128 * (i + 1)],
                                identity=ident[:]))
                        t_t = em.trun_drain(lambda t: t.nop() if hasattr(t, "nop") else t.engine_nop())
                        em.wait("vector", "tE", t_t)
                        t_e = em.run("vector", lambda v, pt=pt, grp=grp, j=j: v.tensor_copy(
                            out=wTsb[:, :].rearrange("p (c w) -> p c w", w=WSL)
                            [:, grp * 8:grp * 8 + 8, 128 * j:128 * (j + 1)],
                            in_=pt[:].rearrange("p (c w) -> p c w", w=128)), sem="vE")
                        em.wait("tensor", "vE", t_e)
                t_v = em.run("vector", lambda v: v.engine_nop(), sem="vE")
                em.wait("gpsimd", "vE", t_v)
                t_g = None
                for half in range(2):
                    hh = nd // 2
                    t_g = em.dma("gpsimd",
                                 wT_in[m].ap().rearrange("(c p) w -> p c w", p=128)
                                 [:, hh * half:hh * (half + 1)],
                                 wTsb[:, :].rearrange("p (c w) -> p c w", w=WSL)
                                 [:, hh * half:hh * (half + 1)], "gS")
                em.wait("gpsimd", "gS", t_g)
                t_ag = em.run("gpsimd", lambda g, m=m: g.collective_compute(
                    "AllGather", ALU.bypass, replica_groups=[list(range(8))],
                    ins=[wT_in[m].ap().opt()], outs=[wT_out[m].ap().opt()]), sem="cc")
                em.wait("gpsimd", "cc", t_ag)
                return t_ag

            for m in ("q", "k", "v", "o"):
                cc_tgt[m] = preprocess_weight(m, False)
            cc_tgt["f"] = preprocess_weight("f", True)

        st01 = ExitStack()
        qT_sb = st01.enter_context(nc.sbuf_tensor("qT_sb", [128, NH * TOK], BF16))
        with ExitStack() as st0:
            ec0 = st0.enter_context
            xrow = [ec0(nc.sbuf_tensor("xrow0", [128, D], F32))]
            xbf = ec0(nc.sbuf_tensor("xbf", [128, D], BF16))
            xT_sb = ec0(nc.sbuf_tensor("xT_sb", [128, 32 * TOK], BF16))
            p0 = [ec0(nc.psum_tensor("p0_%d" % i, [128, 1024], BF16)) for i in range(2)]
            wstrip = [ec0(nc.sbuf_tensor(f"wstrip{i}", [128, D], BF16)) for i in range(2)]
            pp = [ec0(nc.psum_tensor("pp%d" % i, [128, 512], F32)) for i in range(2)]
            kvrow = ec0(nc.sbuf_tensor("kvrow", [128, 1024], BF16))

            em.barrier("sync")
            em.barrier("gpsimd")
            xrow_free = [None]
            for j in range(NTT):
                buf = xrow[0]
                em.wait("sync", "vE", xrow_free[0])
                t_l = em.dma("sync", buf[:], x_hbm[128 * j:128 * (j + 1), :], "dA")
                em.wait("vector", "dA", t_l)
                t_c = em.run("vector", lambda v, b=buf: v.tensor_copy(out=xbf[:], in_=b[:]), sem="vE")
                xrow_free[0] = t_c
                em.wait("tensor", "vE", t_c)
                for grp in range(ngrp):
                    pt = p0[grp % 2]
                    for k in range(8):
                        i = grp * 8 + k
                        em.run("tensor", lambda t, i=i, k=k, pt=pt: t.transpose(
                            out=pt[:, 128 * k:128 * (k + 1)], in_=xbf[:, 128 * i:128 * (i + 1)],
                            identity=ident[:]))
                    t_t = em.trun_drain(lambda t: t.nop() if hasattr(t, "nop") else t.engine_nop())
                    em.wait("vector", "tE", t_t)
                    t_e = em.run("vector", lambda v, pt=pt, grp=grp, j=j: v.tensor_copy(
                        out=xT_sb[:, :].rearrange("p (c w) -> p c w", w=TOK)
                        [:, grp * 8:grp * 8 + 8, 128 * j:128 * (j + 1)],
                        in_=pt[:].rearrange("p (c w) -> p c w", w=128)), sem="vE")
                    em.wait("tensor", "vE", t_e)

            tap("xT", xT_sb[:, 0:512], [128, 512], BF16, waits=[("vE", em.cur("vE"))])
            wstrip_free = [None, None]

            def proj(m, sink):
                em.wait("sync", "cc", cc_tgt[m])
                for c in range(nd):
                    blk, col = (c * 128) // WSL, (c * 128) % WSL
                    sb = wstrip[c % 2]
                    em.wait("sync", "tE", wstrip_free[c % 2])
                    t_l = None
                    for qtr in range(2):
                        hh = nd // 2
                        t_l = em.dma("sync",
                                     sb[:, :].rearrange("p (ci w) -> p ci w", w=128)
                                     [:, hh * qtr:hh * (qtr + 1)],
                                     wT_out[m].ap()[blk * D:(blk + 1) * D, col:col + 128]
                                     .rearrange("(ci p) w -> p ci w", p=128)
                                     [:, hh * qtr:hh * (qtr + 1)], "dA")
                    em.wait("tensor", "dA", t_l)
                    for half in range(TOK // 512 if TOK >= 512 else 1):
                        psum = pp[half]
                        hw = min(512, TOK)
                        t_mm = None
                        for di in range(nd):
                            t_mm = em.run("tensor", lambda t, sb=sb, di=di, half=half, psum=psum, hw=hw: t.matmul(
                                out=psum[:, 0:hw], lhsT=sb[:, 128 * di:128 * (di + 1)],
                                rhs=xT_sb[:, di * TOK + hw * half: di * TOK + hw * half + hw],
                                start=(di == 0), stop=(di == nd - 1)), sem="tE")
                        wstrip_free[c % 2] = t_mm
                        sink(c, half, psum, t_mm)

            def q_sink(c, half, psum, t_mm):
                em.wait("vector", "tE", t_mm)
                hw = min(512, TOK)
                t_s = em.run("vector", lambda v, psum=psum, c=c, half=half, hw=hw: v.tensor_scalar(
                    out=qT_sb[:, c * TOK + hw * half: c * TOK + hw * (half + 1)],
                    in0=psum[:, 0:hw], scalar1=bqkv["q"][:, c:c + 1], scalar2=None, op0=ALU.add), sem="vE")
                em.wait("tensor", "vE", t_s)
            proj("q", q_sink)
            tap("wstrip", wstrip[1][:, 0:512], [128, 512], BF16, waits=[("tE", em.cur("tE"))])
            tap("qT", qT_sb[:, 0:512], [128, 512], BF16, waits=[("vE", em.cur("vE"))])

            def k_sink(c, half, psum, t_mm):
                em.wait("vector", "tE", t_mm)
                em.wait("vector", "gS", em.cur("gS"))
                hw = min(512, TOK)
                t_s = em.run("vector", lambda v, psum=psum, c=c, hw=hw: v.tensor_scalar(
                    out=kvrow[:, 0:hw], in0=psum[:, 0:hw], scalar1=bqkv["k"][:, c:c + 1],
                    scalar2=None, op0=ALU.add), sem="vE")
                em.wait("tensor", "vE", t_s)
                em.wait("gpsimd", "vE", t_s)
                em.dma("gpsimd", kT_in.ap()[c * 128:(c + 1) * 128, hw * half:hw * (half + 1)],
                       kvrow[:, 0:hw], "gS")
            proj("k", k_sink)

            def v_sink(c, half, psum, t_mm):
                em.wait("vector", "tE", t_mm)
                em.wait("vector", "gS", em.cur("gS"))
                hw = min(512, TOK)
                t_s = em.run("vector", lambda v, psum=psum, hw=hw: v.tensor_scalar(
                    out=kvrow[:, 0:hw], in0=psum[:, 0:hw], scalar1=0.0, scalar2=None,
                    op0=ALU.add), sem="vE")
                em.wait("gpsimd", "vE", t_s)
                em.dma("gpsimd", v_in.ap()[c * 128:(c + 1) * 128, hw * half:hw * (half + 1)],
                       kvrow[:, 0:hw], "gS")
            proj("v", v_sink)
            em.wait("gpsimd", "gS", em.cur("gS"))
            cc_k = em.run("gpsimd", lambda g: g.collective_compute(
                "AllGather", ALU.bypass, replica_groups=[[0, 1, 2, 3], [4, 5, 6, 7]],
                ins=[kT_in.ap().opt()], outs=[kT_out.ap().opt()]), sem="cc")
            tap("vin", v_in.ap(), [D, TOK], BF16, eng="gpsimd", waits=[("gS", em.cur("gS"))])
            cc_v = em.run("gpsimd", lambda g: g.collective_compute(
                "AllGather", ALU.bypass, replica_groups=[[0, 1, 2, 3], [4, 5, 6, 7]],
                ins=[v_in.ap().opt()], outs=[v_out.ap().opt()]), sem="cc")
            em.wait("gpsimd", "cc", cc_v)

        gG = min(G, NTT)
        with ExitStack() as st1:
            ec1 = st1.enter_context
            kT_h = ec1(nc.sbuf_tensor("kT_h", [128, SEQ], BF16))
            v_h = ec1(nc.sbuf_tensor("v_h", [128, SEQ], BF16))
            s_g = [ec1(nc.sbuf_tensor(f"s_g{i}", [128, SEQ], F32)) for i in range(min(G, NTT))]
            es2 = [ec1(nc.sbuf_tensor(f"es2_{i}", [128, SEQ], BF16)) for i in range(2)]
            zpT = ec1(nc.sbuf_tensor("zpT", [128, SEQ], BF16))
            ctx_t = [ec1(nc.sbuf_tensor(f"ctx_t{i}", [128, 128], BF16)) for i in range(2)]
            qcols = 512 * max(1, (SEQ // 8) // 512)
            sqd = ec1(nc.sbuf_tensor("sqd", [128, qcols], BF16))
            ksum = ec1(nc.sbuf_tensor("ksum", [128, 2], F32))
            musq = ec1(nc.sbuf_tensor("musq", [128, 2 * gG], F32))
            ksumb = ec1(nc.sbuf_tensor("ksumb", [128, 512], BF16))
            stt = {n: ec1(nc.sbuf_tensor(f"st_{n}", [128, gG],
                                         mybir.dt.int32 if n in ("ge", "lt") else F32))
                   for n in ("lo", "hi", "mid", "cnt", "ge", "lt", "mu", "sd", "sum", "s2", "mq")}
            ntg = ec1(nc.sbuf_tensor("ntg", [128, gG], F32))
            zrow = ec1(nc.sbuf_tensor("zrow", [128, gG], F32))
            rz = ec1(nc.sbuf_tensor("rz", [128, 1], F32))
            lo_all = ec1(nc.sbuf_tensor("lo_all", [128, NH * NTT], F32)) if DBG else None
            z_all = ec1(nc.sbuf_tensor("z_all", [128, NH * NTT], F32)) if DBG else None
            p_s = [ec1(nc.psum_tensor("p1s%d" % i, [128, 512], F32)) for i in range(4)]
            p_t2 = [ec1(nc.psum_tensor("p1t%d" % i, [128, 1024], BF16)) for i in range(2)]
            p_o = [ec1(nc.psum_tensor("p1o%d" % i, [128, 128], F32)) for i in range(2)]

            em.wait("sync", "cc", cc_v)
            em.barrier("sync")
            em.barrier("gpsimd")
            em.run("vector", lambda v: v.memset(ksumb[:], 0.0))
            em.vd()
            tap("vout", v_out.ap(), [4 * D, TOK], BF16, eng="gpsimd", waits=[("cc", cc_v)])

            npk = min(8, nkv)
            ngv = max(1, nkv // 8)
            ngq = NTT // gG
            st_p1 = {"kq_done": None, "t_k": None, "t_v": None}
            tr_prev = [None, None]
            ctx_dma = [None, None]

            def load_kv(h):
                em.wait("sync", "tE", st_p1["kq_done"])
                t_k = None
                for r in range(4):
                    t_k = em.dma("sync", kT_h[:, TOK * r:TOK * (r + 1)],
                                 kT_out.ap()[r * D + h * 128: r * D + (h + 1) * 128, :], "dK")
                st_p1["t_k"] = t_k
                em.wait("sync", "tE", tr_prev[0])
                t_v = None
                for r in range(4):
                    t_v = em.dma("sync", es2[0][:, TOK * r:TOK * (r + 1)],
                                 v_out.ap()[r * D + h * 128: r * D + (h + 1) * 128, :], "dV")
                st_p1["t_v"] = t_v

            def stage_v(h):
                em.wait("tensor", "dV", st_p1["t_v"])
                ev = [None, None]
                for grp in range(ngv):
                    if ev[grp % 2] is not None:
                        em.wait("tensor", "vE", ev[grp % 2])
                    for k in range(npk):
                        kc = grp * npk + k
                        em.run("tensor", lambda t, kc=kc, k=k, grp=grp: t.transpose(
                            out=p_t2[grp % 2][:, 128 * k:128 * (k + 1)],
                            in_=es2[0][:, 128 * kc:128 * (kc + 1)], identity=ident[:]))
                    t_t2 = em.trun_drain(lambda t: t.nop() if hasattr(t, "nop") else t.engine_nop())
                    em.wait("vector", "tE", t_t2)
                    ev[grp % 2] = em.run("vector", lambda v, grp=grp: v.tensor_copy(
                        out=v_h[:, 128 * npk * grp:128 * npk * (grp + 1)],
                        in_=p_t2[grp % 2][:, 0:128 * npk]), sem="vE")
                em.wait("tensor", "vE", ev[(ngv - 1) % 2])
                if ngv > 1:
                    em.wait("tensor", "vE", ev[ngv % 2])
                tr_prev[0] = em.cur("tE")
                if h == 0:
                    tap("kTh", kT_h[:, 0:512], [128, 512], BF16, waits=[("dK", st_p1["t_k"])])
                    tap("vh", v_h[:, 0:512], [128, 512], BF16, waits=[("vE", em.cur("vE"))])

            def emit_scores(h, gq, guards):
                em.wait("tensor", "dK", st_p1["t_k"])
                if gq == 0:
                    em.wait("vector", "dK", st_p1["t_k"])
                    em.run("vector", lambda v: v.tensor_reduce(
                        out=ksum[:, 0:1], in_=kT_h[:, 0:SEQ], axis=AXX, op=ALU.add))
                    em.run("vector", lambda v: v.tensor_reduce(
                        out=ksum[:, 1:2], in_=kT_h[:, 0:qcols], axis=AXX, op=ALU.add))
                    em.vd()
                    t_ks = em.run("vector", lambda v: v.tensor_copy(
                        out=ksumb[:, 0:2], in_=ksum[:]), sem="vE")
                    em.wait("tensor", "vE", t_ks)
                nc4 = min(4, nkc)
                spl = max(1, nc4 // 2)
                for g in range(gG):
                    qt = gq * gG + g
                    kc0_evac = None
                    for rnd in range(max(1, nkc // 4)):
                        if rnd == 0:
                            em.wait("tensor", "vE", st_p1.get("musq_cp"))
                        t_mm = None
                        for kc4 in range(nc4):
                            kc = rnd * nc4 + kc4
                            t_mm = em.run("tensor", lambda t, h=h, qt=qt, kc=kc, kc4=kc4: t.matmul(
                                out=p_s[kc4][:],
                                lhsT=qT_sb[:, h * TOK + 128 * qt: h * TOK + 128 * (qt + 1)],
                                rhs=kT_h[:, 512 * kc:512 * (kc + 1)],
                                start=True, stop=True), sem="tE")
                        em.wait("scalar", "tE", t_mm)
                        em.wait("vector", "tE", t_mm)
                        if rnd == 0 and guards is not None:
                            em.wait("scalar", "vE", guards[0][g])
                            em.wait("vector", "sE", guards[1][g])
                        t_s = None
                        t_v = None
                        for kc4 in range(nc4):
                            kc = rnd * nc4 + kc4
                            if kc4 < spl:
                                t_v = em.run("vector", lambda v, g=g, kc=kc, kc4=kc4: v.tensor_copy(
                                    out=s_g[g][:, 512 * kc:512 * (kc + 1)], in_=p_s[kc4][:]),
                                    sem="vE")
                                if kc == 0:
                                    kc0_evac = t_v
                            else:
                                t_s = em.run("scalar", lambda s, g=g, kc=kc, kc4=kc4: s.activation(
                                    out=s_g[g][:, 512 * kc:512 * (kc + 1)], in_=p_s[kc4][:],
                                    func=ACT.Copy), sem="sE")
                        if t_s is not None:
                            em.wait("tensor", "sE", t_s)
                        em.wait("tensor", "vE", t_v)
                    t_mu = em.run("tensor", lambda t, h=h, qt=qt: t.matmul(
                        out=p_s[0][:],
                        lhsT=qT_sb[:, h * TOK + 128 * qt: h * TOK + 128 * (qt + 1)],
                        rhs=ksumb[:], start=True, stop=True), sem="tE")
                    em.wait("vector", "tE", t_mu)
                    st_p1["musq_cp"] = em.run("vector", lambda v, g=g: v.tensor_copy(
                        out=musq[:, 2 * g:2 * g + 2], in_=p_s[0][:, 0:2]), sem="vE")
                    em.wait("scalar", "vE", kc0_evac)
                    sm = "sE" if g == gG - 1 else None
                    t_sq = em.run("scalar", lambda s, g=g: s.activation(
                        out=sqd[:], in_=s_g[g][:, 0:qcols], func=ACT.Square,
                        accum_out=stt["s2"][:, g:g + 1]), sem=sm)
                st_p1["kq_done"] = em.cur("tE")
                return t_sq

            def emit_stats(t_sq, first):
                em.wait("vector", "sE", t_sq)
                for g in range(gG):
                    em.run("vector", lambda v, g=g: v.tensor_copy(
                        out=stt["sum"][:, g:g + 1], in_=musq[:, 2 * g:2 * g + 1]))
                    em.run("vector", lambda v, g=g: v.tensor_copy(
                        out=stt["mq"][:, g:g + 1], in_=musq[:, 2 * g + 1:2 * g + 2]))
                em.vd()
                if first:
                    tap("s0", s_g[0][:, 0:SEQ], [128, SEQ], F32, waits=[("sE", t_sq)])
                em.run("vector", lambda v: v.tensor_scalar(
                    out=stt["mu"][:], in0=stt["sum"][:], scalar1=1.0 / SEQ, scalar2=None,
                    op0=ALU.mult))
                em.run("vector", lambda v: v.tensor_scalar(
                    out=stt["mq"][:], in0=stt["mq"][:], scalar1=1.0 / qcols, scalar2=None,
                    op0=ALU.mult))
                em.vd()
                em.run("vector", lambda v: v.tensor_tensor(
                    out=stt["mid"][:], in0=stt["mq"][:], in1=stt["mq"][:], op=ALU.mult))
                em.run("vector", lambda v: v.tensor_scalar(
                    out=stt["sd"][:], in0=stt["s2"][:], scalar1=1.0 / qcols, scalar2=None,
                    op0=ALU.mult))
                em.vd()
                em.run("vector", lambda v: v.tensor_tensor(
                    out=stt["sd"][:], in0=stt["sd"][:], in1=stt["mid"][:], op=ALU.subtract))
                em.vd()
                t_v = em.run("vector", lambda v: v.tensor_scalar(
                    out=stt["sd"][:], in0=stt["sd"][:], scalar1=1e-12, scalar2=None,
                    op0=ALU.max), sem="vE")
                em.wait("scalar", "vE", t_v)
                t_s = em.run("scalar", lambda s: s.activation(
                    out=stt["sd"][:], in_=stt["sd"][:], func=ACT.Sqrt), sem="sE")
                em.wait("vector", "sE", t_s)
                em.run("vector", lambda v: v.tensor_scalar(
                    out=stt["lo"][:], in0=stt["sd"][:], scalar1=0.65, scalar2=None, op0=ALU.mult))
                em.run("vector", lambda v: v.tensor_scalar(
                    out=stt["hi"][:], in0=stt["sd"][:], scalar1=1.05, scalar2=None, op0=ALU.mult))
                em.vd()
                em.run("vector", lambda v: v.tensor_tensor(
                    out=stt["lo"][:], in0=stt["lo"][:], in1=stt["mu"][:], op=ALU.add))
                em.run("vector", lambda v: v.tensor_tensor(
                    out=stt["hi"][:], in0=stt["hi"][:], in1=stt["mu"][:], op=ALU.add))
                em.vd()

            def emit_probes():
                gsp = gG // 2
                for it in range(NPROBE):
                    em.run("vector", lambda v: v.tensor_tensor(
                        out=stt["mid"][:], in0=stt["lo"][:], in1=stt["hi"][:], op=ALU.add))
                    em.vd()
                    t_mid = em.run("vector", lambda v: v.tensor_scalar(
                        out=stt["mid"][:], in0=stt["mid"][:], scalar1=0.5, scalar2=None,
                        op0=ALU.mult), sem="vE")
                    t_scnt = None
                    if gsp > 0:
                        em.wait("scalar", "vE", t_mid)
                        for g in range(gG - gsp, gG):
                            em.run("scalar", lambda s, g=g: s.activation(
                                out=es2[1][:], in_=s_g[g][:], func=ACT.Sign,
                                bias=stt["mid"][:, g:g + 1], scale=-1.0,
                                accum_out=stt["sum"][:, g:g + 1]))
                        t_scnt = em.run("scalar", lambda s, gsp=gsp: s.activation(
                            out=stt["cnt"][:, gG - gsp:gG], in_=stt["sum"][:, gG - gsp:gG],
                            func=ACT.Copy, bias=float(SEQ) / 2.0, scale=-0.5), sem="sE")
                    for g in range(gG - gsp):
                        em.run("vector", lambda v, g=g: v.tensor_scalar(
                            out=es2[0][:], in0=s_g[g][:], scalar1=stt["mid"][:, g:g + 1],
                            scalar2=None, op0=ALU.is_ge, op1=ALU.add,
                            accum_out=stt["cnt"][:, g:g + 1]))
                    em.vd()
                    if t_scnt:
                        em.wait("vector", "sE", t_scnt)
                    em.run("vector", lambda v: v.tensor_scalar(
                        out=stt["ge"][:], in0=stt["cnt"][:], scalar1=float(KTOP),
                        scalar2=None, op0=ALU.is_ge))
                    em.run("vector", lambda v: v.tensor_scalar(
                        out=stt["lt"][:], in0=stt["cnt"][:], scalar1=float(KTOP),
                        scalar2=None, op0=ALU.is_lt))
                    em.vd()
                    em.run("vector", lambda v: v.copy_predicated(
                        out=stt["lo"][:], mask=stt["ge"][:], data=stt["mid"][:]))
                    em.run("vector", lambda v: v.copy_predicated(
                        out=stt["hi"][:], mask=stt["lt"][:], data=stt["mid"][:]))
                    em.vd()

            def emit_epilogue(h, gq, boundary, first):
                t_nt = em.run("vector", lambda v: v.tensor_scalar(
                    out=ntg[:], in0=stt["lo"][:], scalar1=-ISC, scalar2=None,
                    op0=ALU.mult), sem="vE")
                em.wait("scalar", "vE", t_nt)
                if first:
                    tap("lo", stt["lo"][:], [128, gG], F32, waits=[("vE", t_nt)])
                    tap("mu", stt["mu"][:], [128, gG], F32, waits=[("vE", t_nt)])
                    tap("sd", stt["sd"][:], [128, gG], F32, waits=[("vE", t_nt)])
                    tap("cnt", stt["cnt"][:], [128, gG], F32, waits=[("vE", t_nt)])
                exp_done = [None] * gG
                pop_done = [None] * gG
                for g in range(min(2, gG)):
                    if tr_prev[g % 2] is not None:
                        em.wait("scalar", "tE", tr_prev[g % 2])
                    exp_done[g] = em.run("scalar", lambda s, g=g: s.activation(
                        out=es2[g % 2][:], in_=s_g[g][:], func=ACT.Exp,
                        bias=ntg[:, g:g + 1], scale=ISC), sem="sE")
                zpc = [None, None]
                for g in range(gG):
                    qt = gq * gG + g
                    em.wait("vector", "sE", exp_done[g])
                    pop_done[g] = em.run("vector", lambda v, g=g: v.scalar_tensor_tensor(
                        out=es2[g % 2][:], in0=s_g[g][:], scalar=stt["lo"][:, g:g + 1],
                        in1=es2[g % 2][:], op0=ALU.is_ge, op1=ALU.mult,
                        accum_out=zrow[:, g:g + 1]), sem="vE")
                    em.run("vector", lambda v, g=g: v.reciprocal(out=rz[:], in_=zrow[:, g:g + 1]))
                    em.vd()
                    t_norm = em.run("vector", lambda v, g=g: v.tensor_scalar(
                        out=es2[g % 2][:], in0=es2[g % 2][:], scalar1=rz[:], scalar2=None,
                        op0=ALU.mult), sem="vE")
                    pc = st_p1.get("pend_ctx")
                    if pc is not None:
                        ph, pqt, ppar, ppv = pc
                        em.wait("vector", "tE", ppv)
                        if ctx_dma[ppar] is not None:
                            em.wait("vector", "gS", ctx_dma[ppar])
                        t_oe = em.run("vector", lambda v, ph=ph, ppar=ppar: v.tensor_scalar(
                            out=ctx_t[ppar][:], in0=p_o[ppar][:], scalar1=bqkv["v"][:, ph:ph + 1],
                            scalar2=None, op0=ALU.add), sem="vE")
                        st_p1["ctx_vE%d" % ppar] = t_oe
                        em.wait("gpsimd", "vE", t_oe)
                        ctx_dma[ppar] = em.dma(
                            "gpsimd",
                            ctxT_hbm.ap()[ph * 128:(ph + 1) * 128, pqt * 128:(pqt + 1) * 128],
                            ctx_t[ppar][:], "gS")
                    em.wait("tensor", "vE", t_norm)
                    for grp in range(ngv):
                        if zpc[grp % 2] is not None:
                            em.wait("tensor", "sE", zpc[grp % 2])
                        for k in range(npk):
                            kc = grp * npk + k
                            em.run("tensor", lambda t, kc=kc, k=k, g=g, grp=grp: t.transpose(
                                out=p_t2[grp % 2][:, 128 * k:128 * (k + 1)],
                                in_=es2[g % 2][:, 128 * kc:128 * (kc + 1)], identity=ident[:]))
                        t_t = em.trun_drain(lambda t: t.nop() if hasattr(t, "nop") else t.engine_nop())
                        em.wait("scalar", "tE", t_t)
                        zpc[grp % 2] = em.run("scalar", lambda s, grp=grp: s.activation(
                            out=zpT[:, 128 * npk * grp:128 * npk * (grp + 1)],
                            in_=p_t2[grp % 2][:, 0:128 * npk], func=ACT.Copy), sem="sE")
                    tr_prev[g % 2] = em.cur("tE")
                    if g + 2 < gG:
                        g2 = g + 2
                        em.wait("scalar", "tE", tr_prev[g % 2])
                        exp_done[g2] = em.run("scalar", lambda s, g2=g2: s.activation(
                            out=es2[g2 % 2][:], in_=s_g[g2][:], func=ACT.Exp,
                            bias=ntg[:, g2:g2 + 1], scale=ISC), sem="sE")
                    em.wait("tensor", "sE", zpc[(ngv - 1) % 2])
                    cv = st_p1.get("ctx_vE%d" % (g % 2))
                    if cv is not None:
                        em.wait("tensor", "vE", cv)
                    t_mm = None
                    for kc in range(nkv):
                        t_mm = em.run("tensor", lambda t, kc=kc, g=g: t.matmul(
                            out=p_o[g % 2][:], lhsT=v_h[:, 128 * kc:128 * (kc + 1)],
                            rhs=zpT[:, 128 * kc:128 * (kc + 1)],
                            start=(kc == 0), stop=(kc == nkv - 1)), sem="tE")
                    st_p1["pend_ctx"] = (h, qt, g % 2, t_mm)
                    if DBG:
                        em.run("vector", lambda v, h=h, gq=gq, g=g: v.tensor_copy(
                            out=z_all[:, (h * ngq + gq) * gG + g:(h * ngq + gq) * gG + g + 1],
                            in_=zrow[:, g:g + 1]))
                    if first and g == 0:
                        tap("zrow", zrow[:], [128, gG], F32, waits=[("vE", em.cur("vE"))])
                if DBG:
                    em.run("vector", lambda v, h=h, gq=gq: v.tensor_copy(
                        out=lo_all[:, (h * ngq + gq) * gG:(h * ngq + gq + 1) * gG],
                        in_=stt["lo"][:]))
                if boundary and h + 1 < NH:
                    load_kv(h + 1)
                    stage_v(h + 1)
                return pop_done, exp_done

            load_kv(0)
            stage_v(0)
            t_sq = emit_scores(0, 0, None)
            for h in range(NH):
                for gq in range(ngq):
                    first = (h == 0 and gq == 0)
                    emit_stats(t_sq, first)
                    emit_probes()
                    boundary = (gq == ngq - 1)
                    pop_done = emit_epilogue(h, gq, boundary, first)
                    if not (h == NH - 1 and boundary):
                        nh, ngq2 = (h, gq + 1) if not boundary else (h + 1, 0)
                        t_sq = emit_scores(nh, ngq2, pop_done)

            pc = st_p1.get("pend_ctx")
            if pc is not None:
                ph, pqt, ppar, ppv = pc
                em.wait("vector", "tE", ppv)
                if ctx_dma[ppar] is not None:
                    em.wait("vector", "gS", ctx_dma[ppar])
                t_oe = em.run("vector", lambda v, ph=ph, ppar=ppar: v.tensor_scalar(
                    out=ctx_t[ppar][:], in0=p_o[ppar][:], scalar1=bqkv["v"][:, ph:ph + 1],
                    scalar2=None, op0=ALU.add), sem="vE")
                em.wait("gpsimd", "vE", t_oe)
                ctx_dma[ppar] = em.dma(
                    "gpsimd", ctxT_hbm.ap()[ph * 128:(ph + 1) * 128, pqt * 128:(pqt + 1) * 128],
                    ctx_t[ppar][:], "gS")

            if DBG:
                t_fin = em.run("vector", lambda v: v.engine_nop(), sem="vE")
                tap("loall", lo_all[:], [128, NH * NTT], F32, waits=[("vE", t_fin)])
                tap("zall", z_all[:], [128, NH * NTT], F32, waits=[("vE", t_fin)])

        st01.close()
        r_sb = ec(nc.sbuf_tensor("r_sb", [128, NTT * D], F32))
        ctx_done = em.cur("gS")

        with ExitStack() as st2:
            ec2 = st2.enter_context
            lhs_b = [ec2(nc.sbuf_tensor(f"lhs_b{i}", [128, 512], BF16)) for i in range(2)]
            wb1 = [ec2(nc.sbuf_tensor(f"wb1_{i}", [128, 512], BF16)) for i in range(2)]
            wb2 = [ec2(nc.sbuf_tensor(f"wb2_{i}", [128, 512], BF16)) for i in range(2)]
            scr = ec2(nc.sbuf_tensor("scr", [128, D], F32))
            scr8 = ec2(nc.sbuf_tensor("scr8", [128, D], FP8))
            scrb = ec2(nc.sbuf_tensor("scrb", [128, D], BF16))
            mstat = ec2(nc.sbuf_tensor("mstat", [128, NTT], F32))
            rstat = ec2(nc.sbuf_tensor("rstat", [128, NTT], F32))
            amx = ec2(nc.sbuf_tensor("amx", [128, 1], F32))
            amx2 = ec2(nc.sbuf_tensor("amx2", [128, 1], F32))

            def matmul_sweep(lhsT_hbm, bias_sb, wkey, evac, lhs_guard, bias_guard):
                with ExitStack() as stS:
                    pmm = [stS.enter_context(nc.psum_tensor(f"pS_{wkey}{i}", [128, 512], F32))
                           for i in range(8)]
                    em.wait("sync", "gS", lhs_guard)
                    em.barrier("sync", sems=("tE", "vE", "sE"))
                    em.wait("tensor", "vE", bias_guard)
                    lfree = [None, None]
                    wfree = [None, None]
                    nquad = max(1, NTT // 4)
                    ntq = min(4, NTT)
                    npair = NCORES // 2
                    for quad in range(nquad):
                        for pair in range(npair):
                            r1, r2 = 2 * pair, 2 * pair + 1
                            t_mm = None
                            for di in range(nd):
                                par = di % 2
                                em.wait("sync", "tE", lfree[par])
                                em.dma("sync", lhs_b[par][:, 0:128 * ntq],
                                       lhsT_hbm[di * 128:(di + 1) * 128,
                                                quad * 128 * ntq:(quad + 1) * 128 * ntq], "dO")
                                em.wait("sync", "tE", wfree[par])
                                em.dma("sync", wb1[par][:, 0:WSL],
                                       wT_out[wkey].ap()[r1 * D + di * 128: r1 * D + (di + 1) * 128, :],
                                       "dO")
                                t_l = em.dma("sync", wb2[par][:, 0:WSL],
                                             wT_out[wkey].ap()[r2 * D + di * 128: r2 * D + (di + 1) * 128, :],
                                             "dO")
                                em.wait("tensor", "dO", t_l)
                                for tt4 in range(ntq):
                                    em.run("tensor", lambda t, par=par, tt4=tt4, di=di: t.matmul(
                                        out=pmm[2 * tt4][:, 0:WSL],
                                        lhsT=lhs_b[par][:, 128 * tt4:128 * (tt4 + 1)],
                                        rhs=wb1[par][:, 0:WSL], start=(di == 0), stop=False))
                                    t_mm = em.run("tensor", lambda t, par=par, tt4=tt4, di=di: t.matmul(
                                        out=pmm[2 * tt4 + 1][:, 0:WSL],
                                        lhsT=lhs_b[par][:, 128 * tt4:128 * (tt4 + 1)],
                                        rhs=wb2[par][:, 0:WSL], start=(di == 0), stop=False), sem="tE")
                                lfree[par] = t_mm
                                wfree[par] = t_mm
                            for tt4 in range(ntq):
                                em.run("tensor", lambda t, r1=r1, tt4=tt4: t.matmul(
                                    out=pmm[2 * tt4][:, 0:WSL], lhsT=ones_row[:],
                                    rhs=bias_sb[0:1, WSL * r1:WSL * (r1 + 1)],
                                    start=False, stop=True))
                                t_mm = em.run("tensor", lambda t, r2=r2, tt4=tt4: t.matmul(
                                    out=pmm[2 * tt4 + 1][:, 0:WSL], lhsT=ones_row[:],
                                    rhs=bias_sb[0:1, WSL * r2:WSL * (r2 + 1)],
                                    start=False, stop=True), sem="tE")
                            for tt4 in range(ntq):
                                evac(quad * ntq + tt4, pair, pmm[2 * tt4], pmm[2 * tt4 + 1], t_mm)
                    em.wait("tensor", "vE", em.cur("vE"))

            def o_evac(tt, pair, pa, pb, t_mm):
                em.wait("vector", "tE", t_mm)
                t_v = None
                for which, pq in ((0, pa), (1, pb)):
                    dcol = (2 * pair + which) * WSL
                    t_v = em.run("vector", lambda v, pq=pq, tt=tt, dcol=dcol: v.tensor_copy(
                        out=r_sb[:, tt * D + dcol: tt * D + dcol + WSL], in_=pq[:, 0:WSL]), sem="vE")
                em.wait("tensor", "vE", t_v)
            tap("ctxT", ctxT_hbm.ap(), [D, TOK], BF16, eng="gpsimd",
                waits=[("gS", ctx_done)])
            matmul_sweep(ctxT_hbm.ap(), bo_sb, "o", o_evac, ctx_done, bo_cast_v)
            tap("osum", r_sb[:, 0:NTT * D], [128, NTT * D], F32, waits=[("vE", em.cur("vE"))])

            scr_free = None
            for tt in range(NTT):
                em.wait("sync", "vE", scr_free)
                t_l = em.dma("sync", scr[:], x_hbm[tt * 128:(tt + 1) * 128, :], "dO")
                em.wait("vector", "dO", t_l)
                t_add = em.run("vector", lambda v, tt=tt: v.tensor_tensor(
                    out=r_sb[:, tt * D:(tt + 1) * D], in0=r_sb[:, tt * D:(tt + 1) * D],
                    in1=scr[:], op=ALU.add), sem="vE")
                em.vd()
                em.run("vector", lambda v, tt=tt: v.tensor_reduce(
                    out=mstat[:, tt:tt + 1], in_=r_sb[:, tt * D:(tt + 1) * D], axis=AXX, op=ALU.add))
                em.wait("scalar", "vE", t_add)
                t_s = em.run("scalar", lambda s, tt=tt: s.activation(
                    out=scrb[:], in_=r_sb[:, tt * D:(tt + 1) * D], func=ACT.Square,
                    accum_out=rstat[:, tt:tt + 1]), sem="sE")
                em.wait("vector", "sE", t_s)
                em.vd()
                em.run("vector", lambda v, tt=tt: v.tensor_scalar(
                    out=mstat[:, tt:tt + 1], in0=mstat[:, tt:tt + 1], scalar1=1.0 / D,
                    scalar2=None, op0=ALU.mult))
                em.vd()
                em.run("vector", lambda v, tt=tt: v.tensor_tensor(
                    out=amx2[:], in0=mstat[:, tt:tt + 1], in1=mstat[:, tt:tt + 1], op=ALU.mult))
                em.run("vector", lambda v, tt=tt: v.tensor_scalar(
                    out=rstat[:, tt:tt + 1], in0=rstat[:, tt:tt + 1], scalar1=1.0 / D,
                    scalar2=None, op0=ALU.mult))
                em.vd()
                em.run("vector", lambda v, tt=tt: v.tensor_tensor(
                    out=rstat[:, tt:tt + 1], in0=rstat[:, tt:tt + 1], in1=amx2[:], op=ALU.subtract))
                em.vd()
                t_v = em.run("vector", lambda v, tt=tt: v.tensor_scalar(
                    out=rstat[:, tt:tt + 1], in0=rstat[:, tt:tt + 1], scalar1=1e-5,
                    scalar2=None, op0=ALU.add), sem="vE")
                em.wait("scalar", "vE", t_v)
                t_s = em.run("scalar", lambda s, tt=tt: s.activation(
                    out=rstat[:, tt:tt + 1], in_=rstat[:, tt:tt + 1], func=ACT.Sqrt), sem="sE")
                em.wait("vector", "sE", t_s)
                em.run("vector", lambda v, tt=tt: v.reciprocal(
                    out=rstat[:, tt:tt + 1], in_=rstat[:, tt:tt + 1]))
                em.vd()
                em.run("vector", lambda v, tt=tt: v.tensor_scalar(
                    out=scr[:], in0=r_sb[:, tt * D:(tt + 1) * D], scalar1=mstat[:, tt:tt + 1],
                    scalar2=None, op0=ALU.subtract))
                scr_free = em.run("vector", lambda v: v.tensor_reduce(
                    out=amx2[:], in_=scr[:], axis=AXX, op=ALU.max, apply_absolute_value=True), sem="vE")
                em.vd()
                em.run("vector", lambda v, tt=tt: v.tensor_tensor(
                    out=amx2[:], in0=amx2[:], in1=rstat[:, tt:tt + 1], op=ALU.mult))
                em.vd()
                if tt == 0:
                    em.run("vector", lambda v: v.tensor_copy(out=amx[:], in_=amx2[:]))
                else:
                    em.run("vector", lambda v: v.tensor_tensor(
                        out=amx[:], in0=amx[:], in1=amx2[:], op=ALU.max))
                em.vd()

            tap("r1", r_sb[:, 0:NTT * D], [128, NTT * D], F32, waits=[("vE", em.cur("vE"))])
            tap("mst", mstat[:], [128, NTT], F32, waits=[("vE", em.cur("vE"))])
            tap("rst", rstat[:], [128, NTT], F32, waits=[("vE", em.cur("vE"))])
            tap("amx", amx[:], [128, 1], F32, waits=[("vE", em.cur("vE"))])
            t_v = em.run("vector", lambda v: v.engine_nop(), sem="vE")
            em.wait("gpsimd", "vE", t_v)
            t_g = em.dma("gpsimd", red_hbm.ap(), amx[:], "gS")
            em.wait("gpsimd", "gS", t_g)
            t_g = em.dma("gpsimd", red_sb[:], red_hbm.ap(), "gS")
            em.wait("vector", "gS", t_g)
            em.run("vector", lambda v: v.tensor_reduce(
                out=one_sc[0:1, 1:2], in_=red_sb[0:1, :], axis=AXX, op=ALU.max))
            em.vd()
            em.run("vector", lambda v: v.tensor_copy(
                out=one_sc[0:1, 0:8], in_=one_sc[0:1, 1:2].to_broadcast([1, 8])))
            t_v = em.run("vector", lambda v: v.engine_nop(), sem="vE")
            em.wait("gpsimd", "vE", t_v)
            t_g = em.dma("gpsimd", ax_in[0:1, 0:8], one_sc[0:1, 0:8], "gS")
            em.wait("gpsimd", "gS", t_g)
            t_cc = em.run("gpsimd", lambda g: g.collective_compute(
                "AllReduce", ALU.max, replica_groups=[list(range(8))],
                ins=[ax_in.ap().opt()], outs=[ax_out.ap().opt()]), sem="cc")
            em.wait("gpsimd", "cc", t_cc)
            t_g = em.dma("gpsimd", sxi2[:], ax_out[0:1, 0:1].to_broadcast([128, 1]), "gS")
            em.wait("vector", "gS", t_g)
            em.run("vector", lambda v: v.tensor_scalar(
                out=sxi2[:], in0=sxi2[:], scalar1=1e-12, scalar2=1.0 / 448.0,
                op0=ALU.max, op1=ALU.mult))
            em.vd()
            em.run("vector", lambda v: v.tensor_tensor(
                out=dq_sc[:], in0=sxi2[:], in1=sw_sc[:], op=ALU.mult))
            em.vd()
            em.run("vector", lambda v: v.tensor_scalar(
                out=dq_sc[:], in0=dq_sc[:], scalar1=4.0, scalar2=None, op0=ALU.mult))
            em.run("vector", lambda v: v.tensor_scalar(
                out=sxi2[:], in0=sxi2[:], scalar1=2.0, scalar2=None, op0=ALU.mult))
            em.vd()
            em.run("vector", lambda v: v.reciprocal(out=sxi2[:], in_=sxi2[:]))
            em.run("vector", lambda v: v.reciprocal(out=amx2[:], in_=dq_sc[:]))
            em.vd()
            bfs_v = em.run("vector", lambda v: v.tensor_scalar(
                out=bfs_sb[:], in0=bfr_sb[:], scalar1=amx2[0:1, 0:1], scalar2=None,
                op0=ALU.mult), sem="vE")

            with ExitStack() as stq:
                ptr2 = stq.enter_context(nc.psum_tensor("ptr2", [128, 1024], BF16))
                for tt in range(NTT):
                    em.wait("vector", "gS", em.cur("gS"))
                    em.run("vector", lambda v, tt=tt: v.tensor_scalar(
                        out=scr[:], in0=r_sb[:, tt * D:(tt + 1) * D], scalar1=mstat[:, tt:tt + 1],
                        scalar2=rstat[:, tt:tt + 1], op0=ALU.subtract, op1=ALU.mult))
                    em.vd()
                    em.run("vector", lambda v, tt=tt: v.tensor_copy(
                        out=r_sb[:, tt * D:(tt + 1) * D], in_=scr[:]))
                    em.run("vector", lambda v: v.tensor_scalar(
                        out=scr8[:], in0=scr[:], scalar1=sxi2[:], scalar2=None, op0=ALU.mult))
                    em.vd()
                    t_c = em.run("vector", lambda v: v.tensor_copy(out=scrb[:], in_=scr8[:]), sem="vE")
                    em.wait("tensor", "vE", t_c)
                    for grp in range(ngrp):
                        for k in range(8):
                            i = grp * 8 + k
                            em.run("tensor", lambda t, i=i, k=k: t.transpose(
                                out=ptr2[:, 128 * k:128 * (k + 1)], in_=scrb[:, 128 * i:128 * (i + 1)],
                                identity=ident[:]))
                        t_t = em.trun_drain(lambda t: t.nop() if hasattr(t, "nop") else t.engine_nop())
                        em.wait("vector", "tE", t_t)
                        t_e = em.run("vector", lambda v, grp=grp: v.tensor_copy(
                            out=scrb[:, 1024 * grp:1024 * (grp + 1)], in_=ptr2[:]), sem="vE")
                        em.wait("tensor", "vE", t_e)
                    em.wait("gpsimd", "vE", em.cur("vE"))
                    for half in range(2):
                        hh = nd // 2
                        em.dma("gpsimd",
                               qxT_hbm.ap()[:, tt * 128:(tt + 1) * 128]
                               .rearrange("(c p) w -> p c w", p=128)
                               [:, hh * half:hh * (half + 1)],
                               scrb[:, :].rearrange("p (c w) -> p c w", w=128)
                               [:, hh * half:hh * (half + 1)], "gS")

            tap("sxdq", dq_sc[:], [128, 1], F32, waits=[("vE", em.cur("vE"))])
            tap("y0", r_sb[:, 0:NTT * D], [128, NTT * D], F32, waits=[("vE", em.cur("vE"))])
            def f_evac(tt, pair, pa, pb, t_mm):
                em.wait("vector", "tE", t_mm)
                t_v = None
                for which, pq in ((0, pa), (1, pb)):
                    dcol = (2 * pair + which) * WSL
                    t_v = em.run("vector", lambda v, pq=pq, tt=tt, dcol=dcol: v.scalar_tensor_tensor(
                        out=r_sb[:, tt * D + dcol: tt * D + dcol + WSL], in0=pq[:, 0:WSL],
                        scalar=dq_sc[:], in1=r_sb[:, tt * D + dcol: tt * D + dcol + WSL],
                        op0=ALU.mult, op1=ALU.add), sem="vE")
                em.wait("tensor", "vE", t_v)
            matmul_sweep(qxT_hbm.ap(), bfs_sb, "f", f_evac, em.cur("gS"), bfs_v)

            tap("r2", r_sb[:, 0:NTT * D], [128, NTT * D], F32, waits=[("vE", em.cur("vE"))])
            for tt in range(NTT):
                em.run("vector", lambda v, tt=tt: v.tensor_reduce(
                    out=mstat[:, tt:tt + 1], in_=r_sb[:, tt * D:(tt + 1) * D], axis=AXX, op=ALU.add))
                t_v = em.run("vector", lambda v: v.engine_nop(), sem="vE")
                em.wait("scalar", "vE", t_v)
                t_s = em.run("scalar", lambda s, tt=tt: s.activation(
                    out=scrb[:], in_=r_sb[:, tt * D:(tt + 1) * D], func=ACT.Square,
                    accum_out=rstat[:, tt:tt + 1]), sem="sE")
                em.wait("vector", "sE", t_s)
                em.vd()
                em.run("vector", lambda v, tt=tt: v.tensor_scalar(
                    out=mstat[:, tt:tt + 1], in0=mstat[:, tt:tt + 1], scalar1=1.0 / D,
                    scalar2=None, op0=ALU.mult))
                em.vd()
                em.run("vector", lambda v, tt=tt: v.tensor_tensor(
                    out=amx2[:], in0=mstat[:, tt:tt + 1], in1=mstat[:, tt:tt + 1], op=ALU.mult))
                em.run("vector", lambda v, tt=tt: v.tensor_scalar(
                    out=rstat[:, tt:tt + 1], in0=rstat[:, tt:tt + 1], scalar1=1.0 / D,
                    scalar2=None, op0=ALU.mult))
                em.vd()
                em.run("vector", lambda v, tt=tt: v.tensor_tensor(
                    out=rstat[:, tt:tt + 1], in0=rstat[:, tt:tt + 1], in1=amx2[:], op=ALU.subtract))
                em.vd()
                t_v = em.run("vector", lambda v, tt=tt: v.tensor_scalar(
                    out=rstat[:, tt:tt + 1], in0=rstat[:, tt:tt + 1], scalar1=1e-5,
                    scalar2=None, op0=ALU.add), sem="vE")
                em.wait("scalar", "vE", t_v)
                t_s = em.run("scalar", lambda s, tt=tt: s.activation(
                    out=rstat[:, tt:tt + 1], in_=rstat[:, tt:tt + 1], func=ACT.Sqrt), sem="sE")
                em.wait("vector", "sE", t_s)
                em.run("vector", lambda v, tt=tt: v.reciprocal(
                    out=rstat[:, tt:tt + 1], in_=rstat[:, tt:tt + 1]))
                em.vd()
                em.wait("vector", "dOut", em.cur("dOut"))
                t_v = em.run("vector", lambda v, tt=tt: v.tensor_scalar(
                    out=scr[:], in0=r_sb[:, tt * D:(tt + 1) * D], scalar1=mstat[:, tt:tt + 1],
                    scalar2=rstat[:, tt:tt + 1], op0=ALU.subtract, op1=ALU.mult), sem="vE")
                em.wait("sync", "vE", t_v)
                em.dma("sync", out_hbm[tt * 128:(tt + 1) * 128, :], scr[:], "dOut")

            em.wait("sync", "dOut", em.cur("dOut"))

        with nc.Block() as block:
            @block.sync
            def _(e):
                for fn in em.ops["sync"]:
                    fn(e)

            @block.gpsimd
            def _(e):
                for fn in em.ops["gpsimd"]:
                    fn(e)

            @block.vector
            def _(e):
                for fn in em.ops["vector"]:
                    fn(e)

            @block.scalar
            def _(e):
                for fn in em.ops["scalar"]:
                    fn(e)

            @block.tensor
            def _(e):
                for fn in em.ops["tensor"]:
                    fn(e)

    return nc


_NC_CACHE = None


def kernel(L0_attn_hidden_states, L0_attn_wq, L0_attn_bq, L0_attn_wk, L0_attn_bk,
           L0_attn_wv, L0_attn_bv, L0_attn_wo, L0_attn_bo, L0_ffn_W, L0_ffn_b):
    global _NC_CACHE
    x = np.ascontiguousarray(np.asarray(L0_attn_hidden_states, dtype=np.float32)).reshape(B * SEQ, D)
    ws = {"q": np.asarray(L0_attn_wq, np.float32), "k": np.asarray(L0_attn_wk, np.float32),
          "v": np.asarray(L0_attn_wv, np.float32), "o": np.asarray(L0_attn_wo, np.float32),
          "f": np.asarray(L0_ffn_W, np.float32)}
    bq = np.ascontiguousarray(np.asarray(L0_attn_bq, np.float32).reshape(D, 1))
    bk = np.ascontiguousarray(np.asarray(L0_attn_bk, np.float32).reshape(D, 1))
    bv = np.ascontiguousarray(np.asarray(L0_attn_bv, np.float32).reshape(D, 1))
    bo = np.ascontiguousarray(np.asarray(L0_attn_bo, np.float32).reshape(1, D))
    bf = np.ascontiguousarray(np.asarray(L0_ffn_b, np.float32).reshape(1, D))

    if _NC_CACHE is None:
        _NC_CACHE = build_nc()
    nc = _NC_CACHE

    in_maps = []
    for c in range(NCORES):
        m = {"x_slab": np.ascontiguousarray(x[c * TOK:(c + 1) * TOK]),
             "bq_col": bq, "bk_col": bk, "bv_col": bv, "bo_row": bo, "bf_row": bf}
        for k, w in ws.items():
            m[f"w{k}_sl"] = np.ascontiguousarray(w[c * WSL:(c + 1) * WSL])
        in_maps.append(m)

    res = run_bass_kernel_spmd(nc, in_maps, core_ids=list(range(NCORES)))
    out = np.concatenate([np.asarray(r["out_slab"]) for r in res.results], axis=0)
    return np.ascontiguousarray(out.reshape(B, SEQ, D).astype(np.float32))
